# version 18
# baseline (speedup 1.0000x reference)
import os
os.environ.setdefault("GOMEMLIMIT", "30GiB")
os.environ.setdefault("GOGC", "25")
import numpy as np
import ml_dtypes
import concourse.bass as bass
import concourse.mybir as mybir
from concourse.bass_utils import run_bass_kernel_spmd
from concourse.bass import IndirectOffsetOnAxis

F32 = mybir.dt.float32
BF16 = mybir.dt.bfloat16
I32 = mybir.dt.int32
AF = mybir.ActivationFunctionType
ALU = mybir.AluOpType
BF = ml_dtypes.bfloat16

B, T, D = 2, 16384, 1024
DEBUG_STASH = False
DIN, S, K, R = 2048, 16, 4, 64
NL = 4
CORE_T = 4096
W = 1024
TEXT = CORE_T + W
TC = 512
NCHUNK = TEXT // TC
NWARM = W // TC
TCS = 512
NSUB = TC // TCS
NDB = DIN // 128
NDC = D // 128
FF = 4096
EXPN = 8
NROW = 8192
DUMP = 4096

ENGINES = ("sync", "vector", "scalar", "tensor", "gpsimd")


class Buf:
    __slots__ = ("name", "writers", "readers")

    def __init__(self, name):
        self.name = name
        self.writers = {}
        self.readers = {}


class Prog:

    def __init__(self):
        self.items = {e: [] for e in ENGINES}
        self.count = {}
        self.watermark = {e: {} for e in ENGINES}
        self.ninstr = 0
        for e in ENGINES:
            self.count[e] = 0

    def _bump(self, sem, inc):
        self.count[sem] = self.count.get(sem, 0) + inc
        return self.count[sem]

    def _wait(self, engine, sem, val):
        if val <= 0 or self.watermark[engine].get(sem, 0) >= val:
            return
        self.watermark[engine][sem] = val
        self.items[engine].append(("wait", sem, val))
        self.ninstr += 1

    def _pre(self, engine, reads, writes, skip_sem=None):
        for b in reads:
            for s, v in b.writers.items():
                self._wait(engine, s, v)
        for b in writes:
            for s, v in b.writers.items():
                if s == skip_sem or s == engine:
                    continue
                self._wait(engine, s, v)
            for s, v in b.readers.items():
                if s == engine:
                    continue
                self._wait(engine, s, v)

    def _post(self, sem, reads, writes, inc):
        v = self._bump(sem, inc)
        for b in writes:
            b.writers = {sem: v}
            b.readers = {}
        for b in reads:
            b.readers[sem] = v
        return (sem, v)

    def op(self, engine, fn, reads=(), writes=(), inc=1):
        self._pre(engine, reads, writes)
        tok = self._post(engine, reads, writes, inc)
        self.items[engine].append(("op", [fn], inc, engine))
        self.ninstr += 1
        return tok

    def group(self, engine, fns, reads=(), writes=(), inc=1):
        self._pre(engine, reads, writes)
        tok = self._post(engine, reads, writes, inc)
        self.items[engine].append(("op", list(fns), inc, engine))
        self.ninstr += len(fns)
        return tok

    def dmaop(self, engine, fn, reads=(), writes=()):
        prim = writes[0] if writes else reads[0]
        sem = "b:" + prim.name
        self._pre(engine, reads, writes, skip_sem=sem if writes else None)
        tok = self._post(sem, reads, writes, 16)
        self.items[engine].append(("op", [fn], 16, sem))
        self.ninstr += 1
        return tok


def build_core_program(moe_segs):
    nc = bass.Bass("TRN2", target_bir_lowering=False, debug=False)
    P = Prog()
    h = {}

    def inp(name, shape, dtype=F32):
        h[name] = nc.dram_tensor(name, list(shape), dtype, kind="ExternalInput").ap()
        return h[name]

    inp("xfwT", (D, TEXT))
    inp("xbwT", (D, TEXT))
    inp("xown", (CORE_T, D))
    inp("inWT", (2 * NL, D, 2 * DIN))
    inp("outWT", (2 * NL, DIN, D))
    inp("xpWTp", (2 * NL, DIN, 128))
    inp("dtWT", (2 * NL, R, DIN))
    inp("convw", (2 * NL, 128, NDB * K))
    inp("convb", (2 * NL, 128, NDB))
    inp("dtb", (2 * NL, 128, NDB))
    inp("Dpv", (2 * NL, 128, NDB))
    inp("Alogp", (2 * NL, 128, NDB * S))
    inp("nwv", (2 * NL, 128, NDC))
    inp("nbv", (2 * NL, 128, NDC))
    inp("normfw", (128, NDC))
    inp("normfb", (128, NDC))
    inp("mergeWT", (2 * D, D))
    inp("mergeb", (1, D))
    inp("ffnb", (1, D))
    inp("xrowsT", (D, NROW))
    inp("w1s", (EXPN, D, FF))
    inp("w2s", (EXPN, FF, D))
    inp("rowscale", (128, NROW // 128))
    inp("scatidx0", (128, NROW // 128), I32)
    inp("scatidx1", (128, NROW // 128), I32)
    inp("ident", (128, 128))
    inp("onesv", (128, 1))
    inp("masks", (128, 4))
    inp("constv", (128, 2))
    out_h = nc.dram_tensor("out", [CORE_T, D], F32, kind="ExternalOutput").ap()
    if DEBUG_STASH:
        dbg_fw = nc.dram_tensor("dbg_fw", [D, CORE_T], F32, kind="ExternalOutput").ap()
        dbg_bw = nc.dram_tensor("dbg_bw", [D, CORE_T], F32, kind="ExternalOutput").ap()
        dbg_g = nc.dram_tensor("dbg_g", [CORE_T, D], F32, kind="ExternalOutput").ap()

    def scr(name, shape, dtype=BF16):
        ap = nc.dram_tensor(name, list(shape), dtype, kind="Internal").ap()
        return ap

    winWT = scr("winWT", (2 * NL, D, 2 * DIN))
    woutWT = scr("woutWT", (2 * NL, DIN, D))
    wxpWT = scr("wxpWT", (2 * NL, DIN, 128))
    wdtWT = scr("wdtWT", (2 * NL, R, DIN))
    wconvd = scr("wconvd", (2 * NL, NDB * K, 128, 128))
    wmergeT = scr("wmergeT", (2 * D, D))
    w1b = scr("w1b", (EXPN, D, FF))
    w2b = scr("w2b", (EXPN, FF, D))
    bcrow = scr("bcrow", (2, 32 * TC), F32)
    statrow = scr("statrow", (4, TC), F32)
    fwstash = scr("fwstash", (D, CORE_T))
    bwstash = scr("bwstash", (D, CORE_T))
    xnrows = scr("xnrows", (D, NROW))
    hrows = scr("hrows", (FF, NROW))
    moeFM = scr("moeFM", (D, NROW), F32)
    gscat0 = scr("gscat0", (DUMP + 1, D), F32)
    gscat1 = scr("gscat1", (DUMP + 1, D), F32)

    bufs = {}

    def dbuf(name):
        key = "D:" + name
        if key not in bufs:
            bufs[key] = Buf(key)
        return bufs[key]

    def sb(name, shape, dtype=BF16):
        t = nc.alloc_sbuf_tensor(name, list(shape), dtype)
        bufs[name] = Buf(name)
        return t.ap()

    ident_b = sb("ident_b", (128, 128))
    ones_b = sb("ones_b", (128, 1))
    masks_s = sb("masks_s", (128, 4), F32)
    Aneg = sb("Aneg", (128, NDB * S), F32)
    dtb_s = sb("dtb_s", (128, 2 * NL * NDB), F32)
    convb_s = sb("convb_s", (128, 2 * NL * NDB), F32)
    Dp_s = sb("Dp_s", (128, 2 * NL * NDB), F32)
    nw_s = sb("nw_s", (128, 2 * NL * NDC), F32)
    nb_s = sb("nb_s", (128, 2 * NL * NDC), F32)
    nfw_s = sb("nfw_s", (128, NDC), F32)
    nfb_s = sb("nfb_s", (128, NDC), F32)

    rs_s = sb("rs_s", (128, NROW // 128), F32)
    idx0_s = sb("idx0_s", (128, NROW // 128), I32)
    idx1_s = sb("idx1_s", (128, NROW // 128), I32)
    hcarry = sb("hcarry", (128, NL * NDB * S), F32)
    utail = sb("utail", (128, NL * NDB * (K - 1)))

    rbuf = sb("rbuf", (128, NDC * TC))
    xhat = sb("xhat", (128, NDC * TC))
    ubuf = sb("ubuf", (128, NDB * TC))
    dAb0 = sb("dAb0", (128, NSUB * S * TCS))
    dAb1 = sb("dAb1", (128, NSUB * S * TCS))
    dBuA = sb("dBuA", (128, S * TCS))
    hbf = sb("hbf", (128, S * TCS))
    Bbc = sb("Bbc", (128, NSUB * S * TCS))
    Cbc = sb("Cbc", (128, NSUB * S * TCS))
    dtublks = [sb("dtublk0", (128, TC)), sb("dtublk1", (128, TC))]
    siluzs = [sb("siluz0", (128, TC)), sb("siluz1", (128, TC))]
    ytmpall = sb("ytmpall", (128, NDB * TC))
    wstage = ytmpall[:, 0:2048]
    wstage2 = ytmpall.bitcast(F32)[:, 1024:3072]
    etmps = [sb("etmp0", (128, TC)), sb("etmp1", (128, TC))]
    etmp = etmps[0]
    mubc = sb("mubc", (128, TC), F32)
    rsbc = sb("rsbc", (128, TC), F32)
    mrow = sb("mrow", (1, TC), F32)
    vrow = sb("vrow", (1, TC), F32)
    sqacc = sb("sqacc", (128, TC))
    wsl_inA = sb("wsl_inA", (128, NDC * 128))
    wsl_inB = sb("wsl_inB", (128, NDC * 128))
    wsl_outA = sb("wsl_outA", (128, 16 * 128))
    wsl_outB = sb("wsl_outB", (128, 16 * 128))
    wsl_xp = sb("wsl_xp", (128, NDB * 128))
    wsl_dt = sb("wsl_dt", (R, NDB * 128))
    wsl_cdA = sb("wsl_cdA", (128, K * 128))
    wsl_cdB = sb("wsl_cdB", (128, K * 128))
    dtr_s = sb("dtr_s", (R, TC))
    fix1 = sb("fix1", (128, S), F32)
    fix2 = sb("fix2", (128, S), F32)
    dtall = sb("dtall", (128, NDB * TC))
    dtsl = []
    for _i in range(NDB):
        bufs[f"dtall_{_i}"] = Buf(f"dtall_{_i}")
        dtsl.append(bufs[f"dtall_{_i}"])
    usl = []
    for _i in range(NDB):
        bufs[f"ubuf_{_i}"] = Buf(f"ubuf_{_i}")
        usl.append(bufs[f"ubuf_{_i}"])
    bufs["ubufdma"] = Buf("ubufdma")
    dBus = [dBuA, ytmpall]
    identf = sb("identf", (128, 128), F32)

    def psum(name, shape, dtype=F32):
        t = nc.alloc_psum_tensor(name, list(shape), dtype)
        bufs[name] = Buf(name)
        return t.ap()

    pAs = [psum("pA0", (128, TC)), psum("pA1", (128, TC))]
    pBs = [psum("pB0", (128, TC)), psum("pB1", (128, TC))]
    pYs = [psum("pY0", (128, TCS)), psum("pY1", (128, TCS))]
    pT = psum("pT", (128, 256))
    pStat = psum("pStat", (1, 512))

    Bf = lambda n: bufs[n]
    gtmp = Bbc.bitcast(F32)[:, 0:3072]
    mergeb_bc = Cbc.bitcast(F32)[:, 0:1024]
    ffnb_bc = Cbc.bitcast(F32)[:, 1024:2048]

    def mmsplit(mk, nk, ncols, reads, writes):
        fns = []
        for c0 in range(0, ncols, 512):
            c1 = min(ncols, c0 + 512)
            for kc in range(nk):
                fns.append(mk(kc, c0, c1, kc == 0, kc == nk - 1))
        P.group("tensor", fns, reads=reads, writes=writes)

    def dma(queue, dst_ap, src_ap, writes, reads):
        return P.dmaop(queue, lambda e: e.dma_start(out=dst_ap, in_=src_ap),
                       reads=list(reads), writes=list(writes))

    dma("gpsimd", ident_b[:, :], h["ident"][:, :], [Bf("ident_b")], [])
    dma("sync", identf[:, :], h["ident"][:, :], [Bf("identf")], [])
    dma("gpsimd", ones_b[:, :], h["onesv"][:, :], [Bf("ones_b")], [])
    dma("sync", masks_s[:, :], h["masks"][:, :], [Bf("masks_s")], [])
    constv_s = sb("constv_s", (128, 2), F32)
    dma("sync", constv_s[:, :], h["constv"][:, :], [Bf("constv_s")], [])
    nc.const_aps.register(F32, 1e-5, constv_s[:, 0:1])
    nc.const_aps.register(F32, -0.5, constv_s[:, 1:2])
    for _e in ENGINES:
        P._wait(_e, "b:constv_s", P.count.get("b:constv_s", 0))
    dma("sync", dtb_s.rearrange("p (l n) -> p l n", l=2 * NL), h["dtb"].rearrange("l p n -> p l n"), [Bf("dtb_s")], [])
    dma("sync", convb_s.rearrange("p (l n) -> p l n", l=2 * NL), h["convb"].rearrange("l p n -> p l n"), [Bf("convb_s")], [])
    dma("sync", Dp_s.rearrange("p (l n) -> p l n", l=2 * NL), h["Dpv"].rearrange("l p n -> p l n"), [Bf("Dp_s")], [])
    dma("sync", nw_s.rearrange("p (l n) -> p l n", l=2 * NL), h["nwv"].rearrange("l p n -> p l n"), [Bf("nw_s")], [])
    dma("sync", nb_s.rearrange("p (l n) -> p l n", l=2 * NL), h["nbv"].rearrange("l p n -> p l n"), [Bf("nb_s")], [])
    dma("sync", nfw_s[:, :], h["normfw"][:, :], [Bf("nfw_s")], [])
    dma("sync", nfb_s[:, :], h["normfb"][:, :], [Bf("nfb_s")], [])

    dma("sync", rs_s[:, :], h["rowscale"][:, :], [Bf("rs_s")], [])
    dma("sync", idx0_s[:, :], h["scatidx0"][:, :], [Bf("idx0_s")], [])
    dma("sync", idx1_s[:, :], h["scatidx1"][:, :], [Bf("idx1_s")], [])


    def zero_carries():
        P.op("vector", lambda e: e.memset(hcarry[:, :], 0.0), writes=[Bf("hcarry")])
        P.op("vector", lambda e: e.memset(utail[:, :], 0.0), writes=[Bf("utail")])

    def cast_weight(src_ap, dst_ap, rows, cols, dstbuf):
        for r0 in range(0, rows, 128):
            rr = min(128, rows - r0)
            for c0 in range(0, cols, 2048):
                cc = min(2048, cols - c0)
                dma("sync", wstage2[:rr, :cc], src_ap[r0:r0 + rr, c0:c0 + cc],
                    [Bf("ytmpall")], [])
                P.op("vector", lambda e, rr=rr, cc=cc: e.tensor_copy(
                    out=wstage[:rr, :cc], in_=wstage2[:rr, :cc]),
                    reads=[Bf("ytmpall")], writes=[Bf("ytmpall")])
                dma("gpsimd", dst_ap[r0:r0 + rr, c0:c0 + cc], wstage[:rr, :cc],
                    [dstbuf], [Bf("ytmpall")])

    for ld in range(2 * NL):
        cast_weight(h["inWT"][ld], winWT[ld], D, 2 * DIN, dbuf("winWT"))
        cast_weight(h["outWT"][ld], woutWT[ld], DIN, D, dbuf("woutWT"))
        cast_weight(h["xpWTp"][ld], wxpWT[ld], DIN, 128, dbuf("wxpWT"))
        cast_weight(h["dtWT"][ld], wdtWT[ld], R, DIN, dbuf("wdtWT"))
    cast_weight(h["mergeWT"], wmergeT, 2 * D, D, dbuf("wmergeT"))
    for ee in range(EXPN):
        cast_weight(h["w1s"][ee], w1b[ee], D, FF, dbuf("w1b"))
        cast_weight(h["w2s"][ee], w2b[ee], FF, D, dbuf("w2b"))

    for ld in range(2 * NL):
        dma("sync", wstage2[:, :NDB * K], h["convw"][ld][:, :], [Bf("ytmpall")], [])
        for col in range(NDB * K):
            P.op("vector", lambda e, col=col: e.tensor_tensor(
                out=wstage[:, :128], in0=ident_b[:, :],
                in1=wstage2[:, col:col + 1].broadcast_to((128, 128)), op=ALU.mult),
                reads=[Bf("ytmpall"), Bf("ident_b")], writes=[Bf("ytmpall")])
            dma("gpsimd", wconvd[ld][col][:, :], wstage[:, :128], [dbuf("wconvd")], [Bf("ytmpall")])

    lnctr = [0]

    def layer_norm(src_fn, src_buf, dst_fn, dst_buf, gamma_fn, beta_fn, eps=1e-5):
        for c0 in range(0, TC, 512):
            c1 = min(TC, c0 + 512)
            fns = [(lambda e, dc=dc, c0=c0, c1=c1: e.matmul(
                out=pStat[:, 0:c1 - c0], lhsT=ones_b[:, :], rhs=src_fn(dc)[:, c0:c1],
                start=(dc == 0), stop=(dc == NDC - 1))) for dc in range(NDC)]
            P.group("tensor", fns, reads=[src_buf, Bf("ones_b")], writes=[Bf("pStat")])
            P.op("vector", lambda e, c0=c0, c1=c1: e.tensor_scalar_mul(
                out=mrow[:, c0:c1], in0=pStat[:, 0:c1 - c0], scalar1=1.0 / D),
                reads=[Bf("pStat")], writes=[Bf("mrow")])
        for c0 in range(0, TC, 512):
            c1 = min(TC, c0 + 512)
            fns = []
            for dc in range(NDC):
                P.op("scalar", lambda e, dc=dc, c0=c0, c1=c1: e.activation(
                    out=sqacc[:, c0:c1], in_=src_fn(dc)[:, c0:c1], func=AF.Square),
                    reads=[src_buf], writes=[Bf("sqacc")])
                P.op("tensor", lambda e, dc=dc, c0=c0, c1=c1: e.matmul(
                    out=pStat[:, 0:c1 - c0], lhsT=ones_b[:, :], rhs=sqacc[:, c0:c1],
                    start=(dc == 0), stop=(dc == NDC - 1)),
                    reads=[Bf("sqacc"), Bf("ones_b")], writes=[Bf("pStat")])
            P.op("vector", lambda e, c0=c0, c1=c1: e.tensor_scalar_mul(
                out=vrow[:, c0:c1], in0=pStat[:, 0:c1 - c0], scalar1=1.0 / D),
                reads=[Bf("pStat")], writes=[Bf("vrow")])
        P.op("vector", lambda e: e.tensor_tensor(
            out=etmp[0:1, :], in0=mrow[:, :], in1=mrow[:, :], op=ALU.mult),
            reads=[Bf("mrow")], writes=[Bf("etmp0")])
        P.op("vector", lambda e: e.tensor_tensor(
            out=vrow[:, :], in0=vrow[:, :], in1=etmp[0:1, :], op=ALU.subtract),
            reads=[Bf("etmp0")], writes=[Bf("vrow")])
        P.op("scalar", lambda e: e.activation(
            out=vrow[:, :], in_=vrow[:, :], func=AF.Ln, bias=eps),
            reads=[], writes=[Bf("vrow")])
        P.op("scalar", lambda e: e.activation(
            out=vrow[:, :], in_=vrow[:, :], func=AF.Exp, scale=-0.5),
            reads=[], writes=[Bf("vrow")])
        sl = lnctr[0] % 2
        lnctr[0] += 1
        dma("gpsimd", statrow[2 * sl:2 * sl + 1, :], mrow[:, :], [], [Bf("mrow")])
        dma("gpsimd", statrow[2 * sl + 1:2 * sl + 2, :], vrow[:, :], [], [Bf("vrow")])
        dma("gpsimd", mubc[:, :], statrow[2 * sl:2 * sl + 1, :].broadcast_to((128, TC)),
            [Bf("mubc")], [])
        dma("gpsimd", rsbc[:, :], statrow[2 * sl + 1:2 * sl + 2, :].broadcast_to((128, TC)),
            [Bf("rsbc")], [])
        for dc in range(NDC):
            P.op("vector", lambda e, dc=dc: e.tensor_tensor(
                out=dst_fn(dc), in0=src_fn(dc), in1=mubc[:, :], op=ALU.subtract),
                reads=[src_buf, Bf("mubc")], writes=[dst_buf])
            P.op("vector", lambda e, dc=dc: e.tensor_tensor(
                out=dst_fn(dc), in0=dst_fn(dc), in1=rsbc[:, :], op=ALU.mult),
                reads=[Bf("rsbc")], writes=[dst_buf])
            if gamma_fn is not None:
                P.op("vector", lambda e, dc=dc: e.tensor_scalar(
                    out=dst_fn(dc), in0=dst_fn(dc),
                    scalar1=gamma_fn(dc), scalar2=beta_fn(dc),
                    op0=ALU.mult, op1=ALU.add),
                    reads=[Bf("nw_s"), Bf("nb_s"), Bf("nfw_s"), Bf("nfb_s")],
                    writes=[dst_buf])

    rview = lambda dc: rbuf[:, dc * TC:(dc + 1) * TC]
    xview = lambda dc: xhat[:, dc * TC:(dc + 1) * TC]

    def run_stack(dname, x_ap, stash_ap, killcol):
        stash_name = dname + "stash"
        zero_carries()
        for c in range(NCHUNK):
            warm = (c < NWARM)
            t0 = c * TC
            for l in range(NL):
                ld = (0 if dname == "fw" else NL) + l
                dma("sync", Aneg[:, :], h["Alogp"][ld:ld + 1, :, :].rearrange("l p n -> p (l n)"),
                    [Bf("Aneg")], [])
                P.op("scalar", lambda e: e.activation(out=Aneg[:, :], in_=Aneg[:, :],
                                                      func=AF.Exp),
                     reads=[], writes=[Bf("Aneg")])
                P.op("vector", lambda e: e.tensor_scalar_mul(out=Aneg[:, :], in0=Aneg[:, :],
                                                             scalar1=-1.0),
                     reads=[], writes=[Bf("Aneg")])
                if l == 0:
                    for dc in range(NDC):
                        dma("gpsimd", rview(dc),
                            x_ap[dc * 128:(dc + 1) * 128, t0:t0 + TC], [Bf("rbuf")], [])
                    layer_norm(rview, Bf("rbuf"), rview, Bf("rbuf"), None, None)
                layer_norm(rview, Bf("rbuf"), xview, Bf("xhat"),
                           (lambda dc, ld=ld: nw_s[:, ld * NDC + dc:ld * NDC + dc + 1]),
                           (lambda dc, ld=ld: nb_s[:, ld * NDC + dc:ld * NDC + dc + 1]))

                wsrc = winWT[ld].rearrange("(kc p) f -> p kc f", p=128)
                for f in range(NDB):
                    wv = [wsl_inA, wsl_inB][f % 2]
                    wvb = Bf(["wsl_inA", "wsl_inB"][f % 2])
                    pA = pAs[f % 2]
                    pAb = Bf(["pA0", "pA1"][f % 2])
                    dma("sync", wv.rearrange("p (kc f) -> p kc f", kc=NDC),
                        wsrc[:, :, f * 128:(f + 1) * 128], [wvb], [dbuf("winWT")])
                    mmsplit(lambda kc, c0, c1, st, sp, wv=wv, pA=pA: (
                        lambda e: e.matmul(
                            out=pA[:, c0:c1], lhsT=wv[:, kc * 128:(kc + 1) * 128],
                            rhs=xview(kc)[:, c0:c1], start=st, stop=sp)),
                        NDC, TC, [wvb, Bf("xhat")], [pAb])
                    if warm:
                        P.op("vector", lambda e, f=f, pA=pA: e.tensor_scalar_mul(
                            out=ubuf[:, f * TC:(f + 1) * TC], in0=pA[:, :],
                            scalar1=masks_s[:, killcol:killcol + 1]),
                            reads=[pAb, Bf("masks_s")], writes=[usl[f], Bf("ubufdma")])
                    else:
                        P.op("vector", lambda e, f=f, pA=pA: e.tensor_copy(
                            out=ubuf[:, f * TC:(f + 1) * TC], in_=pA[:, :]),
                            reads=[pAb], writes=[usl[f], Bf("ubufdma")])

                cdsrc = wconvd[ld].rearrange("n q r -> q n r")
                for db in range(NDB):
                    cdv = [wsl_cdA, wsl_cdB][db % 2]
                    cdvb = Bf(["wsl_cdA", "wsl_cdB"][db % 2])
                    pB = pBs[db % 2]
                    pBb = Bf(["pB0", "pB1"][db % 2])
                    dma("sync", cdv.rearrange("q (n r) -> q n r", n=K),
                        cdsrc[:, db * K:(db + 1) * K, :], [cdvb], [dbuf("wconvd")])
                    fns = []
                    for c0 in range(0, TC, 512):
                        c1 = c0 + 512
                        specs = []
                        for k in range(K):
                            sh = k - (K - 1)
                            lo = max(-sh, c0)
                            specs.append(("main", k, sh, lo))
                            if c0 == 0 and -sh > 0:
                                specs.append(("tail", k, sh, -sh))
                        for i, (kind, k, sh, lo) in enumerate(specs):
                            first, last = (i == 0), (i == len(specs) - 1)
                            if kind == "main":
                                fns.append((lambda e, db=db, k=k, sh=sh, lo=lo, c1=c1,
                                            cdv=cdv, pB=pB, first=first, last=last: e.matmul(
                                    out=pB[:, lo:c1], lhsT=cdv[:, k * 128:(k + 1) * 128],
                                    rhs=ubuf[:, db * TC + lo + sh: db * TC + c1 + sh],
                                    start=first, stop=last)))
                            else:
                                fns.append((lambda e, db=db, k=k, lo=lo, cdv=cdv, l=l,
                                            pB=pB, first=first, last=last: e.matmul(
                                    out=pB[:, 0:lo], lhsT=cdv[:, k * 128:(k + 1) * 128],
                                    rhs=utail[:, (l * NDB + db) * (K - 1) + k:
                                               (l * NDB + db) * (K - 1) + k + lo],
                                    start=first, stop=last)))
                    P.group("tensor", fns,
                            reads=[cdvb, usl[db], Bf("utail")], writes=[pBb])
                    P.op("vector", lambda e, db=db, l=l: e.tensor_copy(
                        out=utail[:, (l * NDB + db) * (K - 1):(l * NDB + db) * (K - 1) + K - 1],
                        in_=ubuf[:, db * TC + TC - (K - 1): db * TC + TC]),
                        reads=[usl[db]], writes=[Bf("utail")])
                    P.op("scalar", lambda e, db=db, ld=ld, pB=pB: e.activation(
                        out=ubuf[:, db * TC:(db + 1) * TC], in_=pB[:, :], func=AF.Silu,
                        bias=convb_s[:, ld * NDB + db: ld * NDB + db + 1]),
                        reads=[pBb, Bf("convb_s"), Bf("utail")], writes=[usl[db]])

                dma("sync", wsl_xp.rearrange("p (kc f) -> p kc f", kc=NDB),
                    wxpWT[ld].rearrange("(kc p) f -> p kc f", p=128), [Bf("wsl_xp")],
                    [dbuf("wxpWT")])
                pA = pAs[0]
                pAb = Bf("pA0")
                mmsplit(lambda kc, c0, c1, st, sp, pA=pA: (
                    lambda e: e.matmul(
                        out=pA[:, c0:c1], lhsT=wsl_xp[:, kc * 128:(kc + 1) * 128],
                        rhs=ubuf[:, kc * TC + c0:kc * TC + c1], start=st, stop=sp)),
                    NDB, TC, [Bf("wsl_xp")] + usl, [pAb])
                P.op("vector", lambda e, pA=pA: e.tensor_copy(out=dtr_s[:, :], in_=pA[64:128, :]),
                     reads=[pAb], writes=[Bf("dtr_s")])
                slbc = c % 2
                P.op("vector", lambda e, pA=pA: e.tensor_copy(out=sqacc[0:16, :], in_=pA[0:16, :]),
                     reads=[pAb], writes=[Bf("sqacc")])
                P.op("vector", lambda e, pA=pA: e.tensor_copy(out=sqacc[32:48, :], in_=pA[32:48, :]),
                     reads=[pAb], writes=[Bf("sqacc")])
                bv = bcrow[slbc:slbc + 1, :]
                dma("gpsimd", bv[:, 0:16 * TC].rearrange("o (s t) -> (o s) t", s=16),
                    sqacc[0:16, :], [], [Bf("sqacc")])
                dma("gpsimd", bv[:, 16 * TC:32 * TC].rearrange("o (s t) -> (o s) t", s=16),
                    sqacc[32:48, :], [], [Bf("sqacc")])
                for m in range(NSUB):
                    dma("gpsimd",
                        Bbc[:, m * S * TCS:(m + 1) * S * TCS]
                        .rearrange("p (s t) -> p s t", s=S),
                        bv[:, 0:16 * TC].rearrange("o (s m t) -> o m s t", s=S, m=NSUB)
                        [:, m, :, :].broadcast_to((128, S, TCS)), [Bf("Bbc")], [])
                    dma("gpsimd",
                        Cbc[:, m * S * TCS:(m + 1) * S * TCS]
                        .rearrange("p (s t) -> p s t", s=S),
                        bv[:, 16 * TC:32 * TC].rearrange("o (s m t) -> o m s t", s=S, m=NSUB)
                        [:, m, :, :].broadcast_to((128, S, TCS)), [Bf("Cbc")], [])
                dma("sync", wsl_dt[:, :], wdtWT[ld][:, :], [Bf("wsl_dt")], [dbuf("wdtWT")])

                hbf3 = hbf.rearrange("p (s t) -> p s t", s=S)
                for db in range(NDB):
                    par = db % 2
                    pB = pBs[par]
                    pBb = Bf(["pB0", "pB1"][par])
                    etmp = etmps[par]
                    etmpb = Bf(["etmp0", "etmp1"][par])
                    mmsplit(lambda kc, c0, c1, st, sp, db=db, pB=pB: (
                        lambda e: e.matmul(
                            out=pB[:, c0:c1], lhsT=wsl_dt[:, db * 128:(db + 1) * 128],
                            rhs=dtr_s[:, c0:c1], start=True, stop=True)),
                        1, TC, [Bf("wsl_dt"), Bf("dtr_s")], [pBb])
                    P.op("scalar", lambda e, db=db, ld=ld, pB=pB, etmp=etmp: e.activation(
                        out=etmp[:, :], in_=pB[:, :], func=AF.Exp,
                        bias=dtb_s[:, ld * NDB + db:ld * NDB + db + 1]),
                        reads=[pBb, Bf("dtb_s")], writes=[etmpb])
                    P.op("scalar", lambda e, db=db, etmp=etmp: e.activation(
                        out=dtall[:, db * TC:(db + 1) * TC], in_=etmp[:, :],
                        func=AF.Ln, bias=1.0),
                        reads=[etmpb], writes=[dtsl[db]])
                def emit_stt(db2):
                    par2 = db2 % 2
                    pY2 = pYs[par2]
                    pYb2 = Bf(["pY0", "pY1"][par2])
                    P.op("vector", lambda e, db=db2, ld=ld, pY=pY2: e.scalar_tensor_tensor(
                        out=ubuf[:, db * TC: (db + 1) * TC],
                        in0=ubuf[:, db * TC: (db + 1) * TC],
                        scalar=Dp_s[:, ld * NDB + db:ld * NDB + db + 1],
                        in1=pY[:, :], op0=ALU.mult, op1=ALU.add),
                        reads=[usl[db2], Bf("Dp_s"), pYb2], writes=[usl[db2]])
                for db in range(NDB):
                    par = db % 2
                    pY = pYs[par]
                    pYb = Bf(["pY0", "pY1"][par])
                    dAb = [dAb0, dAb1][par]
                    dAbb = Bf(["dAb0", "dAb1"][par])
                    dtublk = dtublks[par]
                    dtublkb = Bf(["dtublk0", "dtublk1"][par])
                    dbu = dBus[par]
                    dbub = Bf(["dBuA", "ytmpall"][par])
                    dbu3 = dbu.rearrange("p (s t) -> p s t", s=S)
                    dA4 = dAb.rearrange("p (m s t) -> p m s t", m=NSUB, s=S)
                    dt4 = dtall[:, db * TC:(db + 1) * TC].rearrange("p (m t) -> p m t", m=NSUB)
                    P.op("gpsimd", lambda e, db=db, dtublk=dtublk: e.tensor_tensor(
                        out=dtublk[:, :], in0=dtall[:, db * TC:(db + 1) * TC],
                        in1=ubuf[:, db * TC:(db + 1) * TC], op=ALU.mult),
                        reads=[dtsl[db], usl[db]], writes=[dtublkb])
                    if warm:
                        P.op("gpsimd", lambda e, dtublk=dtublk: e.tensor_scalar_mul(
                            out=dtublk[:, :], in0=dtublk[:, :],
                            scalar1=masks_s[:, killcol:killcol + 1]),
                            reads=[Bf("masks_s")], writes=[dtublkb])
                    for s_ in range(S):
                        P.op("scalar", lambda e, s_=s_, db=db, dA4=dA4, dt4=dt4: e.activation(
                            out=dA4[:, :, s_, :], in_=dt4[:, :, :], func=AF.Exp,
                            scale=Aneg[:, db * S + s_: db * S + s_ + 1]),
                            reads=[dtsl[db], Bf("Aneg")], writes=[dAbb])
                    hc = hcarry[:, (l * NDB + db) * S:(l * NDB + db + 1) * S]
                    for m in range(NSUB):
                        dAm = dAb[:, m * S * TCS:(m + 1) * S * TCS]
                        dAm3 = dAm.rearrange("p (s t) -> p s t", s=S)
                        P.op("gpsimd", lambda e, m=m, dtublk=dtublk, dbu3=dbu3: e.tensor_tensor(
                            out=dbu3,
                            in0=dtublk[:, m * TCS:(m + 1) * TCS]
                            .rearrange("p (o t) -> p o t", o=1).broadcast_to((128, S, TCS)),
                            in1=Bbc[:, m * S * TCS:(m + 1) * S * TCS]
                            .rearrange("p (s t) -> p s t", s=S),
                            op=ALU.mult),
                            reads=[dtublkb, Bf("Bbc")], writes=[dbub])
                        P.op("vector", lambda e, dAm3=dAm3, hc=hc: e.scalar_tensor_tensor(
                            out=fix2[:, 1:S], in0=dAm3[:, 1:S, 0], scalar=1.0,
                            in1=hc[:, 1:S], op0=ALU.mult, op1=ALU.mult),
                            reads=[dAbb, Bf("hcarry")], writes=[Bf("fix2")])
                        P.op("vector", lambda e, dbu3=dbu3: e.tensor_tensor(
                            out=dbu3[:, 1:S, 0], in0=dbu3[:, 1:S, 0], in1=fix2[:, 1:S],
                            op=ALU.add),
                            reads=[Bf("fix2"), dbub], writes=[dbub])
                        P.op("vector", lambda e, dAm3=dAm3: e.memset(dAm3[:, 1:S, 0], 0.0),
                             reads=[], writes=[dAbb])
                        P.op("vector", lambda e, dAm=dAm, dbu=dbu, hc=hc: e.tensor_tensor_scan(
                            out=hbf[:, :], data0=dAm, data1=dbu[:, :],
                            initial=hc[:, 0:1], op0=ALU.mult, op1=ALU.add),
                            reads=[dAbb, dbub, Bf("hcarry")], writes=[Bf("hbf")])
                        P.op("vector", lambda e, hc=hc: e.tensor_copy(
                            out=hc[:, :], in_=hbf3[:, :, TCS - 1]),
                            reads=[Bf("hbf")], writes=[Bf("hcarry")])
                        P.op("vector", lambda e, m=m, dbu=dbu: e.tensor_tensor(
                            out=dbu[:, :], in0=hbf[:, :],
                            in1=Cbc[:, m * S * TCS:(m + 1) * S * TCS], op=ALU.mult),
                            reads=[Bf("hbf"), Bf("Cbc")], writes=[dbub])
                        fns = [(lambda e, s_=s_, pY=pY, dbu=dbu: e.matmul(
                            out=pY[:, :], lhsT=ident_b[:, :],
                            rhs=dbu[:, s_ * TCS:(s_ + 1) * TCS],
                            start=(s_ == 0), stop=(s_ == S - 1))) for s_ in range(S)]
                        P.group("tensor", fns, reads=[dbub, Bf("ident_b")],
                                writes=[pYb])
                    if db > 0:
                        emit_stt(db - 1)
                emit_stt(NDB - 1)

                for db in range(NDB):
                    par = db % 2
                    pB = pBs[par]
                    pBb = Bf(["pB0", "pB1"][par])
                    siluz = siluzs[par]
                    siluzb = Bf(["siluz0", "siluz1"][par])
                    f = NDB + db
                    wv = [wsl_inA, wsl_inB][par]
                    wvb = Bf(["wsl_inA", "wsl_inB"][par])
                    dma("sync", wv.rearrange("p (kc f) -> p kc f", kc=NDC),
                        wsrc[:, :, f * 128:(f + 1) * 128], [wvb], [dbuf("winWT")])
                    mmsplit(lambda kc, c0, c1, st, sp, wv=wv, pB=pB: (
                        lambda e: e.matmul(
                            out=pB[:, c0:c1], lhsT=wv[:, kc * 128:(kc + 1) * 128],
                            rhs=xview(kc)[:, c0:c1], start=st, stop=sp)),
                        NDC, TC, [wvb, Bf("xhat")], [pBb])
                    P.op("scalar", lambda e, pB=pB, siluz=siluz: e.activation(
                        out=siluz[:, :], in_=pB[:, :], func=AF.Silu),
                        reads=[pBb], writes=[siluzb])
                    P.op("vector", lambda e, db=db, siluz=siluz: e.tensor_tensor(
                        out=ubuf[:, db * TC:(db + 1) * TC],
                        in0=ubuf[:, db * TC:(db + 1) * TC], in1=siluz[:, :],
                        op=ALU.mult),
                        reads=[usl[db], siluzb], writes=[usl[db]])

                osrc = woutWT[ld].rearrange("(kc p) f -> p kc f", p=128)
                for f in range(NDC):
                    wv = [wsl_outA, wsl_outB][f % 2]
                    wvb = Bf(["wsl_outA", "wsl_outB"][f % 2])
                    pA = pAs[f % 2]
                    pAb = Bf(["pA0", "pA1"][f % 2])
                    dma("sync", wv.rearrange("p (kc f) -> p kc f", kc=16),
                        osrc[:, :, f * 128:(f + 1) * 128], [wvb], [dbuf("woutWT")])
                    mmsplit(lambda kc, c0, c1, st, sp, wv=wv, pA=pA: (
                        lambda e: e.matmul(
                            out=pA[:, c0:c1], lhsT=wv[:, kc * 128:(kc + 1) * 128],
                            rhs=ubuf[:, kc * TC + c0:kc * TC + c1], start=st, stop=sp)),
                        NDB, TC, [wvb] + usl, [pAb])
                    if l < NL - 1:
                        P.op("vector", lambda e, f=f, pA=pA: e.tensor_tensor(
                            out=rview(f), in0=rview(f), in1=pA[:, :], op=ALU.add),
                            reads=[pAb], writes=[Bf("rbuf")])
                    else:
                        P.op("vector", lambda e, f=f, pA=pA: e.tensor_tensor(
                            out=xview(f), in0=rview(f), in1=pA[:, :], op=ALU.add),
                            reads=[pAb, Bf("rbuf")], writes=[Bf("xhat")])
                if l == NL - 1 and not warm:
                    layer_norm(xview, Bf("xhat"), rview, Bf("rbuf"),
                               (lambda dc: nfw_s[:, dc:dc + 1]),
                               (lambda dc: nfb_s[:, dc:dc + 1]))
                    for dc in range(NDC):
                        dma("gpsimd",
                            stash_ap[dc * 128:(dc + 1) * 128,
                                     (c - NWARM) * TC:(c - NWARM + 1) * TC],
                            rview(dc), [dbuf(stash_name)], [Bf("rbuf")])

    run_stack("fw", h["xfwT"], fwstash, killcol=0)
    run_stack("bw", h["xbwT"], bwstash, killcol=1)

    for rc in range(NROW // TC):
        for dc in range(NDC):
            dma("gpsimd", rview(dc),
                h["xrowsT"][dc * 128:(dc + 1) * 128, rc * TC:(rc + 1) * TC], [Bf("rbuf")], [])
        layer_norm(rview, Bf("rbuf"), rview, Bf("rbuf"), None, None)
        for dc in range(NDC):
            dma("gpsimd", xnrows[dc * 128:(dc + 1) * 128, rc * TC:(rc + 1) * TC],
                rview(dc), [dbuf("xnrows")], [Bf("rbuf")])

    for (ee, r0, r1) in moe_segs:
        n = r1 - r0
        for kc in range(NDC):
            dma("sync", xhat[:, kc * TC:kc * TC + n],
                xnrows[kc * 128:(kc + 1) * 128, r0:r1], [Bf("xhat")], [dbuf("xnrows")])
        w1src = w1b[ee].rearrange("(kc p) f -> p kc f", p=128)
        for f in range(FF // 128):
            wv = [wsl_inA, wsl_inB][f % 2]
            wvb = Bf(["wsl_inA", "wsl_inB"][f % 2])
            pA = pAs[f % 2]
            pAb = Bf(["pA0", "pA1"][f % 2])
            dma("sync", wv.rearrange("p (kc f) -> p kc f", kc=NDC),
                w1src[:, :, f * 128:(f + 1) * 128], [wvb], [dbuf("w1b")])
            fns = [(lambda e, kc=kc, wv=wv, n=n, pA=pA: e.matmul(
                out=pA[:, :n], lhsT=wv[:, kc * 128:(kc + 1) * 128],
                rhs=xhat[:, kc * TC:kc * TC + n],
                start=(kc == 0), stop=(kc == NDC - 1))) for kc in range(NDC)]
            P.group("tensor", fns, reads=[wvb, Bf("xhat")], writes=[pAb])
            P.op("scalar", lambda e, n=n, pA=pA: e.activation(
                out=sqacc[:, :n], in_=pA[:, :n], func=AF.Gelu),
                reads=[pAb], writes=[Bf("sqacc")])
            dma("gpsimd", hrows[f * 128:(f + 1) * 128, r0:r1], sqacc[:, :n],
                [dbuf("hrows")], [Bf("sqacc")])

    for (ee, r0, r1) in moe_segs:
        n = r1 - r0
        for kc in range(32):
            if kc < 16:
                dst = ubuf[:, kc * TC:kc * TC + n]
            else:
                dst = dAb0[:, (kc - 16) * 512:(kc - 16) * 512 + n]
            dma("sync", dst, hrows[kc * 128:(kc + 1) * 128, r0:r1],
                ([Bf("ubufdma"), usl[kc]] if kc < 16 else [Bf("dAb0")]),
                [dbuf("hrows")])
        w2src = w2b[ee].rearrange("(kc p) f -> p kc f", p=128)
        for f in range(NDC):
            pB = pBs[f % 2]
            pBb = Bf(["pB0", "pB1"][f % 2])
            for half in range(2):
                wv = [wsl_outA, wsl_outB][half]
                wvb = Bf(["wsl_outA", "wsl_outB"][half])
                dma("sync", wv.rearrange("p (kc f) -> p kc f", kc=16),
                    w2src[:, half * 16:half * 16 + 16, f * 128:(f + 1) * 128],
                    [wvb], [dbuf("w2b")])
            fns = []
            for kc in range(32):
                wv = [wsl_outA, wsl_outB][kc // 16]
                if kc < 16:
                    mv = ubuf[:, kc * TC:kc * TC + n]
                else:
                    mv = dAb0[:, (kc - 16) * 512:(kc - 16) * 512 + n]
                fns.append((lambda e, kc=kc, wv=wv, mv=mv, n=n, pB=pB: e.matmul(
                    out=pB[:, :n], lhsT=wv[:, (kc % 16) * 128:((kc % 16) + 1) * 128],
                    rhs=mv, start=(kc == 0), stop=(kc == 31))))
            P.group("tensor", fns,
                    reads=[Bf("wsl_outA"), Bf("wsl_outB"), Bf("ubufdma"), Bf("dAb0")],
                    writes=[pBb])
            P.op("vector", lambda e, n=n, pB=pB: e.tensor_copy(out=wstage2[:, :n], in_=pB[:, :n]),
                 reads=[pBb], writes=[Bf("ytmpall")])
            dma("gpsimd", moeFM[f * 128:(f + 1) * 128, r0:r1], wstage2[:, :n],
                [dbuf("moeFM")], [Bf("ytmpall")])

    for rb in range(NROW // 128):
        for f in range(NDC):
            dma("sync", wstage2[:, f * 128:(f + 1) * 128],
                moeFM[f * 128:(f + 1) * 128, rb * 128:(rb + 1) * 128], [Bf("ytmpall")],
                [dbuf("moeFM")])
            P.op("tensor", lambda e, f=f: e.transpose(
                out=pT[:, (f % 2) * 128:(f % 2) * 128 + 128],
                in_=wstage2[:, f * 128:(f + 1) * 128], identity=identf[:, :]),
                reads=[Bf("ytmpall"), Bf("identf")], writes=[Bf("pT")])
            P.op("vector", lambda e, f=f, rb=rb: e.tensor_scalar_mul(
                out=gtmp[:, f * 128:(f + 1) * 128],
                in0=pT[:, (f % 2) * 128:(f % 2) * 128 + 128],
                scalar1=rs_s[:, rb:rb + 1]),
                reads=[Bf("pT"), Bf("rs_s")], writes=[Bf("Bbc")])
        P.dmaop("gpsimd", lambda e, rb=rb: e.indirect_dma_start(
            out=gscat0[:, :], out_offset=IndirectOffsetOnAxis(ap=idx0_s[:, rb:rb + 1], axis=0),
            in_=gtmp[:, 0:1024], in_offset=None),
            reads=[Bf("Bbc"), Bf("idx0_s")], writes=[dbuf("gscat")])
        P.dmaop("gpsimd", lambda e, rb=rb: e.indirect_dma_start(
            out=gscat1[:, :], out_offset=IndirectOffsetOnAxis(ap=idx1_s[:, rb:rb + 1], axis=0),
            in_=gtmp[:, 0:1024], in_offset=None),
            reads=[Bf("Bbc"), Bf("idx1_s")], writes=[dbuf("gscat")])

    gsbuf = dbuf("gscat")

    if DEBUG_STASH:
        for dc in range(NDC):
            for cc in range(2):
                dma("gpsimd", wstage2[:, 0:2048],
                    fwstash[dc * 128:(dc + 1) * 128, cc * 2048:(cc + 1) * 2048],
                    [Bf("ytmpall")], [])
                dma("gpsimd", dbg_fw[dc * 128:(dc + 1) * 128, cc * 2048:(cc + 1) * 2048],
                    wstage2[:, 0:2048], [], [Bf("ytmpall")])
                dma("gpsimd", wstage2[:, 0:2048],
                    bwstash[dc * 128:(dc + 1) * 128, cc * 2048:(cc + 1) * 2048],
                    [Bf("ytmpall")], [])
                dma("gpsimd", dbg_bw[dc * 128:(dc + 1) * 128, cc * 2048:(cc + 1) * 2048],
                    wstage2[:, 0:2048], [], [Bf("ytmpall")])
        for rr in range(CORE_T // 128):
            dma("gpsimd", wstage2[:, 0:1024], gscat0[rr * 128:(rr + 1) * 128, :],
                [Bf("ytmpall")], [gsbuf])
            dma("gpsimd", wstage2[:, 1024:2048], gscat1[rr * 128:(rr + 1) * 128, :],
                [Bf("ytmpall")], [gsbuf])
            P.op("vector", lambda e: e.tensor_tensor(
                out=wstage2[:, 0:1024], in0=wstage2[:, 0:1024], in1=wstage2[:, 1024:2048],
                op=ALU.add), reads=[], writes=[Bf("ytmpall")])
            dma("gpsimd", dbg_g[rr * 128:(rr + 1) * 128, :], wstage2[:, 0:1024],
                [], [Bf("ytmpall")])

    def mslot(kc):
        if kc < 8:
            return ubuf[:, kc * 1024:(kc + 1) * 1024]
        return dAb0[:, (kc - 8) * 1024:(kc - 7) * 1024]

    dma("sync", mergeb_bc, h["mergeb"].broadcast_to((128, D)), [Bf("Cbc")], [])
    dma("sync", ffnb_bc, h["ffnb"].broadcast_to((128, D)), [Bf("Cbc")], [])
    for kc in range(16):
        dma("sync", mslot(kc), wmergeT[kc * 128:(kc + 1) * 128, :],
            ([Bf("ubufdma"), usl[2 * kc], usl[2 * kc + 1]] if kc < 8
             else [Bf("dAb0")]), [dbuf("wmergeT")])
    for blk in range(CORE_T // 128):
        i0 = blk * 128
        for dc in range(NDC):
            dma("sync", wsl_inA[:, dc * 128:(dc + 1) * 128],
                fwstash[dc * 128:(dc + 1) * 128, i0:i0 + 128], [Bf("wsl_inA")],
                [dbuf("fwstash")])
        for dc in range(NDC):
            dma("sync", wsl_outA[:, dc * 128:(dc + 1) * 128],
                bwstash[dc * 128:(dc + 1) * 128, CORE_T - i0 - 128:CORE_T - i0],
                [Bf("wsl_outA")], [dbuf("bwstash")])
            P.op("vector", lambda e, dc=dc: e.tensor_copy(
                out=wsl_inB[:, dc * 128:(dc + 1) * 128],
                in_=wsl_outA[:, dc * 128:(dc + 1) * 128][:, ::-1]),
                reads=[Bf("wsl_outA")], writes=[Bf("wsl_inB")])

        def mlhs(kc):
            if kc < 8:
                return wsl_inA[:, kc * 128:(kc + 1) * 128]
            return wsl_inB[:, (kc - 8) * 128:(kc - 7) * 128]

        for hh in range(2):
            c0 = hh * 512
            pM = pAs[hh]
            pMb = Bf(["pA0", "pA1"][hh])
            fns = [(lambda e, kc=kc, c0=c0, pM=pM: e.matmul(
                out=pM[:, :], lhsT=mlhs(kc),
                rhs=mslot(kc)[:, c0:c0 + 512],
                start=(kc == 0), stop=(kc == 15))) for kc in range(16)]
            P.group("tensor", fns,
                    reads=[Bf("wsl_inA"), Bf("wsl_inB"), Bf("ubufdma"), Bf("dAb0")],
                    writes=[pMb])
        g0 = gtmp[:, 0:1024]
        g1 = gtmp[:, 1024:2048]
        gsum = gtmp[:, 2048:3072]
        gfin = gtmp[:, 2048:3072]
        sgp = gtmp[:, 1024:2048]
        dma("sync", g0, gscat0[i0:i0 + 128, :], [Bf("Bbc")], [gsbuf])
        dma("sync", g1, gscat1[i0:i0 + 128, :], [Bf("Bbc")], [gsbuf])
        P.op("vector", lambda e, g0=g0, g1=g1, gsum=gsum: e.tensor_tensor(
            out=gsum, in0=g0, in1=g1, op=ALU.add),
            reads=[Bf("Bbc")], writes=[Bf("Bbc")])
        P.op("vector", lambda e, gsum=gsum: e.scalar_tensor_tensor(
            out=gtmp[:, 0:1024], in0=gsum, scalar=masks_s[:, 2:3],
            in1=ffnb_bc[:, :], op0=ALU.mult, op1=ALU.add),
            reads=[Bf("masks_s"), Bf("Cbc")], writes=[Bf("Bbc")])
        P.op("scalar", lambda e, gfin=gfin: e.activation(
            out=gfin, in_=gtmp[:, 0:1024], func=AF.Silu),
            reads=[], writes=[Bf("Bbc")])
        P.op("vector", lambda e: e.tensor_tensor(
            out=wstage2[:, 0:512], in0=pAs[0][:, :], in1=mergeb_bc[:, 0:512], op=ALU.add),
            reads=[Bf("pA0"), Bf("Cbc")], writes=[Bf("ytmpall")])
        P.op("vector", lambda e: e.tensor_tensor(
            out=wstage2[:, 512:1024], in0=pAs[1][:, :], in1=mergeb_bc[:, 512:1024],
            op=ALU.add),
            reads=[Bf("pA1"), Bf("Cbc")], writes=[Bf("ytmpall")])
        P.op("vector", lambda e, sgp=sgp, gfin=gfin: e.tensor_tensor(
            out=sgp, in0=wstage2[:, 0:1024], in1=gfin, op=ALU.mult),
            reads=[Bf("ytmpall")], writes=[Bf("Bbc")])
        dma("sync", wstage2[:, 1024:2048], h["xown"][i0:i0 + 128, :], [Bf("ytmpall")], [])
        P.op("vector", lambda e, sgp=sgp: e.tensor_tensor(
            out=wstage2[:, 0:1024], in0=wstage2[:, 1024:2048], in1=sgp, op=ALU.add),
            reads=[Bf("Bbc")], writes=[Bf("ytmpall")])
        dma("gpsimd", out_h[i0:i0 + 128, :], wstage2[:, 0:1024], [], [Bf("ytmpall")])

    P._wait("gpsimd", "b:ytmpall", P.count.get("b:ytmpall", 0))

    import contextlib
    semnames = set(P.count.keys())
    for e in ENGINES:
        for it in P.items[e]:
            if it[0] == "wait":
                semnames.add(it[1])
    with contextlib.ExitStack() as stack:
        sems = {}
        for i, name in enumerate(sorted(semnames)):
            sems[name] = stack.enter_context(nc.semaphore(f"sm{i}"))
        block = stack.enter_context(nc.Block())

        waited = {}
        for e2 in ENGINES:
            for it in P.items[e2]:
                if it[0] == "wait":
                    waited.setdefault(it[1], set()).add(it[2])
        for s2, c2 in P.count.items():
            if c2 > 0:
                waited.setdefault(s2, set()).add(c2)

        def replay(engine, eng):
            running = {}
            pending = {}
            for item in P.items[engine]:
                if item[0] == "wait":
                    _, sem, val = item
                    eng.wait_ge(sems[sem], val)
                else:
                    _, fns, incv, semname = item
                    ins = None
                    for fn in fns:
                        ins = fn(eng)
                    if semname == engine:
                        cnt = running.get(semname, 0) + incv
                        running[semname] = cnt
                        pend = pending.get(semname, 0) + incv
                        if cnt in waited.get(semname, ()):
                            ins.then_inc(sems[semname], pend)
                            pending[semname] = 0
                        else:
                            pending[semname] = pend
                    else:
                        ins.then_inc(sems[semname], incv)

        @block.sync
        def _(eng):
            replay("sync", eng)

        @block.vector
        def _(eng):
            replay("vector", eng)

        @block.scalar
        def _(eng):
            replay("scalar", eng)

        @block.tensor
        def _(eng):
            replay("tensor", eng)

        @block.gpsimd
        def _(eng):
            replay("gpsimd", eng)

    return nc, P


def _host_prep(inputs):
    x = np.asarray(inputs["L0_attn_x"])
    maps = []

    def wstack(pref, name):
        return np.stack([np.asarray(inputs[f"L0_attn_{d}_{name}"][l])
                         for d in ("fw", "bw") for l in range(NL)], axis=0)

    inW = wstack("", "in_W")
    outW = wstack("", "out_W")
    xpW = wstack("", "xproj_W")
    dtW = wstack("", "dt_W")
    convw = wstack("", "conv_w")
    convb = wstack("", "conv_b")
    dtb = wstack("", "dt_b")
    Alog = wstack("", "Alog")
    Dp = wstack("", "D")
    nw = wstack("", "norm_w")
    nb = wstack("", "norm_b")

    inWT = np.ascontiguousarray(inW.transpose(0, 2, 1))
    outWT = np.ascontiguousarray(outW.transpose(0, 2, 1))
    dtWT = np.ascontiguousarray(dtW.transpose(0, 2, 1))
    xpWTp = np.zeros((2 * NL, DIN, 128), np.float32)
    xpWTp[:, :, 0:16] = xpW[:, 64:80].transpose(0, 2, 1)
    xpWTp[:, :, 32:48] = xpW[:, 80:96].transpose(0, 2, 1)
    xpWTp[:, :, 64:128] = xpW[:, 0:64].transpose(0, 2, 1)
    convw_p = np.ascontiguousarray(
        convw.reshape(2 * NL, NDB, 128, K).transpose(0, 2, 1, 3).reshape(2 * NL, 128, NDB * K))

    def packv(v):
        return np.ascontiguousarray(v.reshape(2 * NL, NDB, 128).transpose(0, 2, 1))

    def packd(v):
        return np.ascontiguousarray(v.reshape(2 * NL, NDC, 128).transpose(0, 2, 1))

    Alogp = np.ascontiguousarray(
        Alog.reshape(2 * NL, NDB, 128, S).transpose(0, 2, 1, 3).reshape(2 * NL, 128, NDB * S))
    normfw = np.ascontiguousarray(
        np.asarray(inputs["L0_attn_normf_w"]).reshape(NDC, 128).T)
    normfb = np.ascontiguousarray(
        np.asarray(inputs["L0_attn_normf_b"]).reshape(NDC, 128).T)
    mergeWT = np.ascontiguousarray(np.asarray(inputs["L0_attn_merge_W"]).T)
    mergeb = np.asarray(inputs["L0_attn_merge_b"]).reshape(1, D)
    ffnb = np.asarray(inputs["L0_ffn_bias"]).reshape(1, D)

    ei = np.asarray(inputs["L0_ffn_expert_indices"])
    ew = np.asarray(inputs["L0_ffn_expert_weights"])
    flat_idx = ei.reshape(-1)
    flat_w = ew.reshape(-1)
    order = np.argsort(flat_idx, kind="stable")
    tok = (order // TOPK_CONST).astype(np.int64)
    slot = (order % TOPK_CONST).astype(np.int64)
    xrowsT = np.ascontiguousarray(x[0][tok].T)
    rowscale = np.ascontiguousarray(
        flat_w[order].astype(np.float32).reshape(NROW // 128, 128).T)
    idx0 = np.where(slot == 0, tok, DUMP).astype(np.int32)
    idx1 = np.where(slot == 1, tok, DUMP).astype(np.int32)
    scatidx0 = np.ascontiguousarray(idx0.reshape(NROW // 128, 128).T)
    scatidx1 = np.ascontiguousarray(idx1.reshape(NROW // 128, 128).T)
    counts = np.bincount(flat_idx, minlength=EXPN)
    moe_segs = []
    pos = 0
    for e in range(EXPN):
        end = pos + int(counts[e])
        r = pos
        while r < end:
            r2 = min(r + 512, end)
            moe_segs.append((e, r, r2))
            r = r2
        pos = end

    ident = np.eye(128, dtype=np.float32)
    onesv = np.ones((128, 1), np.float32)

    for c in range(8):
        b, q = c // 4, c % 4
        t_lo, t_hi = q * CORE_T, (q + 1) * CORE_T
        xf = np.zeros((TEXT, D), np.float32)
        lo = max(0, t_lo - W)
        xf[W - (t_lo - lo):W + CORE_T] = x[b, lo:t_hi]
        xfwT = np.ascontiguousarray(xf.T)
        xbw = np.zeros((TEXT, D), np.float32)
        hi = min(T, t_hi + W)
        seg = x[b, t_lo:hi][::-1]
        xbw[TEXT - seg.shape[0]:] = seg
        xbwT = np.ascontiguousarray(xbw.T)
        masks = np.zeros((128, 4), np.float32)
        masks[:, 0] = 0.0 if q == 0 else 1.0
        masks[:, 1] = 0.0 if q == 3 else 1.0
        masks[:, 2] = 1.0 if (b == 0 and q == 0) else 0.0
        maps.append(dict(
            xfwT=xfwT, xbwT=xbwT,
            xown=np.ascontiguousarray(x[b, t_lo:t_hi]),
            inWT=inWT, outWT=outWT, xpWTp=xpWTp, dtWT=dtWT,
            convw=convw_p, convb=packv(convb), dtb=packv(dtb), Dpv=packv(Dp),
            Alogp=Alogp, nwv=packd(nw), nbv=packd(nb),
            normfw=normfw, normfb=normfb,
            mergeWT=mergeWT, mergeb=mergeb, ffnb=ffnb,
            xrowsT=xrowsT,
            w1s=np.asarray(inputs["L0_ffn_w1"]),
            w2s=np.asarray(inputs["L0_ffn_w2"]),
            rowscale=rowscale, scatidx0=scatidx0, scatidx1=scatidx1,
            ident=ident, onesv=onesv, masks=masks,
            constv=np.broadcast_to(np.array([1e-5, -0.5], np.float32), (128, 2)).copy(),
        ))
    return maps, moe_segs


TOPK_CONST = 2


def kernel(**inputs):
    maps, moe_segs = _host_prep(inputs)
    nc, P = build_core_program(moe_segs)
    res = run_bass_kernel_spmd(nc, maps, core_ids=list(range(8)))
    outs = [res.results[c]["out"] for c in range(8)]
    return np.concatenate(outs, axis=0).reshape(B, T, D).astype(np.float32)


# version 21
# speedup vs baseline: 1.0958x; 1.0958x over previous
import os
os.environ.setdefault("GOMEMLIMIT", "30GiB")
os.environ.setdefault("GOGC", "25")
import numpy as np
import ml_dtypes
import concourse.bass as bass
import concourse.mybir as mybir
from concourse.bass_utils import run_bass_kernel_spmd
from concourse.bass import IndirectOffsetOnAxis

F32 = mybir.dt.float32
BF16 = mybir.dt.bfloat16
I32 = mybir.dt.int32
AF = mybir.ActivationFunctionType
ALU = mybir.AluOpType
BF = ml_dtypes.bfloat16

B, T, D = 2, 16384, 1024
DEBUG_STASH = False
DIN, S, K, R = 2048, 16, 4, 64
NL = 4
CORE_T = 4096
W = 1024
TEXT = CORE_T + W
TC = 512
NCHUNK = TEXT // TC
NWARM = W // TC
TCS = 512
NSUB = TC // TCS
NDB = DIN // 128
NDC = D // 128
FF = 4096
EXPN = 8
NROW = 8192
DUMP = 4096

ENGINES = ("sync", "vector", "scalar", "tensor", "gpsimd")


class Buf:
    __slots__ = ("name", "writers", "readers")

    def __init__(self, name):
        self.name = name
        self.writers = {}
        self.readers = {}


class Prog:

    def __init__(self):
        self.items = {e: [] for e in ENGINES}
        self.count = {}
        self.watermark = {e: {} for e in ENGINES}
        self.ninstr = 0
        for e in ENGINES:
            self.count[e] = 0

    def _bump(self, sem, inc):
        self.count[sem] = self.count.get(sem, 0) + inc
        return self.count[sem]

    def _wait(self, engine, sem, val):
        if val <= 0 or self.watermark[engine].get(sem, 0) >= val:
            return
        self.watermark[engine][sem] = val
        self.items[engine].append(("wait", sem, val))
        self.ninstr += 1

    def _pre(self, engine, reads, writes, skip_sem=None):
        for b in reads:
            for s, v in b.writers.items():
                self._wait(engine, s, v)
        for b in writes:
            for s, v in b.writers.items():
                if s == skip_sem or s == engine:
                    continue
                self._wait(engine, s, v)
            for s, v in b.readers.items():
                if s == engine:
                    continue
                self._wait(engine, s, v)

    def _post(self, sem, reads, writes, inc):
        v = self._bump(sem, inc)
        for b in writes:
            b.writers = {sem: v}
            b.readers = {}
        for b in reads:
            b.readers[sem] = v
        return (sem, v)

    def op(self, engine, fn, reads=(), writes=(), inc=1):
        self._pre(engine, reads, writes)
        tok = self._post(engine, reads, writes, inc)
        self.items[engine].append(("op", [fn], inc, engine))
        self.ninstr += 1
        return tok

    def group(self, engine, fns, reads=(), writes=(), inc=1):
        self._pre(engine, reads, writes)
        tok = self._post(engine, reads, writes, inc)
        self.items[engine].append(("op", list(fns), inc, engine))
        self.ninstr += len(fns)
        return tok

    def dmaop(self, engine, fn, reads=(), writes=()):
        prim = writes[0] if writes else reads[0]
        sem = "b:" + prim.name
        self._pre(engine, reads, writes, skip_sem=sem if writes else None)
        tok = self._post(sem, reads, writes, 16)
        self.items[engine].append(("op", [fn], 16, sem))
        self.ninstr += 1
        return tok


def build_core_program(moe_segs):
    nc = bass.Bass("TRN2", target_bir_lowering=False, debug=False)
    P = Prog()
    h = {}

    def inp(name, shape, dtype=F32):
        h[name] = nc.dram_tensor(name, list(shape), dtype, kind="ExternalInput").ap()
        return h[name]

    inp("xfwT", (D, TEXT))
    inp("xbwT", (D, TEXT))
    inp("xown", (CORE_T, D))
    inp("inWT", (2 * NL, D, 2 * DIN))
    inp("outWT", (2 * NL, DIN, D))
    inp("xpWTp", (2 * NL, DIN, 128))
    inp("dtWT", (2 * NL, R, DIN))
    inp("convw", (2 * NL, 128, NDB * K))
    inp("convb", (2 * NL, 128, NDB))
    inp("dtb", (2 * NL, 128, NDB))
    inp("Dpv", (2 * NL, 128, NDB))
    inp("Alogp", (2 * NL, 128, NDB * S))
    inp("nwv", (2 * NL, 128, NDC))
    inp("nbv", (2 * NL, 128, NDC))
    inp("normfw", (128, NDC))
    inp("normfb", (128, NDC))
    inp("mergeWT", (2 * D, D))
    inp("mergeb", (1, D))
    inp("ffnb", (1, D))
    inp("xrowsT", (D, NROW))
    inp("w1s", (EXPN, D, FF))
    inp("w2s", (EXPN, FF, D))
    inp("rowscale", (128, NROW // 128))
    inp("scatidx0", (128, NROW // 128), I32)
    inp("scatidx1", (128, NROW // 128), I32)
    inp("ident", (128, 128))
    inp("onesv", (128, 1))
    inp("masks", (128, 4))
    inp("constv", (128, 2))
    out_h = nc.dram_tensor("out", [CORE_T, D], F32, kind="ExternalOutput").ap()
    if DEBUG_STASH:
        dbg_fw = nc.dram_tensor("dbg_fw", [D, CORE_T], F32, kind="ExternalOutput").ap()
        dbg_bw = nc.dram_tensor("dbg_bw", [D, CORE_T], F32, kind="ExternalOutput").ap()
        dbg_g = nc.dram_tensor("dbg_g", [CORE_T, D], F32, kind="ExternalOutput").ap()

    def scr(name, shape, dtype=BF16):
        ap = nc.dram_tensor(name, list(shape), dtype, kind="Internal").ap()
        return ap

    winWT = scr("winWT", (2 * NL, D, 2 * DIN))
    woutWT = scr("woutWT", (2 * NL, DIN, D))
    wxpWT = scr("wxpWT", (2 * NL, DIN, 128))
    wdtWT = scr("wdtWT", (2 * NL, R, DIN))
    wconvd = scr("wconvd", (2 * NL, NDB * K, 128, 128))
    wmergeT = scr("wmergeT", (2 * D, D))
    w1b = scr("w1b", (EXPN, D, FF))
    w2b = scr("w2b", (EXPN, FF, D))
    bcrow = scr("bcrow", (2, 32 * TC), F32)
    statrow = scr("statrow", (4, TC), F32)
    fwstash = scr("fwstash", (D, CORE_T))
    bwstash = scr("bwstash", (D, CORE_T))
    xnrows = scr("xnrows", (D, NROW))
    hrows = scr("hrows", (FF, NROW))
    moeFM = scr("moeFM", (D, NROW), F32)
    gscat0 = scr("gscat0", (DUMP + 1, D), F32)
    gscat1 = scr("gscat1", (DUMP + 1, D), F32)

    bufs = {}

    def dbuf(name):
        key = "D:" + name
        if key not in bufs:
            bufs[key] = Buf(key)
        return bufs[key]

    def sb(name, shape, dtype=BF16):
        t = nc.alloc_sbuf_tensor(name, list(shape), dtype)
        bufs[name] = Buf(name)
        return t.ap()

    ident_b = sb("ident_b", (128, 128))
    ones_b = sb("ones_b", (128, 1))
    masks_s = sb("masks_s", (128, 4), F32)
    Aneg = sb("Aneg", (128, NDB * S), F32)
    dtb_s = sb("dtb_s", (128, 2 * NL * NDB), F32)
    convb_s = sb("convb_s", (128, 2 * NL * NDB), F32)
    Dp_s = sb("Dp_s", (128, 2 * NL * NDB), F32)
    nw_s = sb("nw_s", (128, 2 * NL * NDC), F32)
    nb_s = sb("nb_s", (128, 2 * NL * NDC), F32)
    nfw_s = sb("nfw_s", (128, NDC), F32)
    nfb_s = sb("nfb_s", (128, NDC), F32)

    rs_s = sb("rs_s", (128, NROW // 128), F32)
    idx0_s = sb("idx0_s", (128, NROW // 128), I32)
    idx1_s = sb("idx1_s", (128, NROW // 128), I32)
    hcarry = sb("hcarry", (128, NL * NDB * S), F32)
    utail = sb("utail", (128, NL * NDB * (K - 1)))

    rbuf = sb("rbuf", (128, NDC * TC))
    xhat = sb("xhat", (128, NDC * TC))
    ubuf = sb("ubuf", (128, NDB * TC))
    dAb0 = sb("dAb0", (128, NSUB * S * TCS))
    dAb1 = sb("dAb1", (128, NSUB * S * TCS))
    dBuA = sb("dBuA", (128, S * TCS))
    hbf = sb("hbf", (128, S * TCS))
    Bbc = sb("Bbc", (128, NSUB * S * TCS))
    Cbc = sb("Cbc", (128, NSUB * S * TCS))
    dtublks = [sb("dtublk0", (128, TC)), sb("dtublk1", (128, TC))]
    siluzs = [sb("siluz0", (128, TC)), sb("siluz1", (128, TC))]
    ytmpall = sb("ytmpall", (128, NDB * TC))
    wstage = ytmpall[:, 0:2048]
    wstage2 = ytmpall.bitcast(F32)[:, 1024:3072]
    etmps = [sb("etmp0", (128, TC)), sb("etmp1", (128, TC))]
    etmp = etmps[0]
    mubc = sb("mubc", (128, TC), F32)
    rsbc = sb("rsbc", (128, TC), F32)
    mrow = sb("mrow", (1, TC), F32)
    vrow = sb("vrow", (1, TC), F32)
    sqacc = sb("sqacc", (128, TC))
    wsl_inA = sb("wsl_inA", (128, NDC * 128))
    wsl_inB = sb("wsl_inB", (128, NDC * 128))
    wsl_outA = sb("wsl_outA", (128, 16 * 128))
    wsl_outB = sb("wsl_outB", (128, 16 * 128))
    wsl_xp = sb("wsl_xp", (128, NDB * 128))
    wsl_dt = sb("wsl_dt", (R, NDB * 128))
    wsl_cdA = sb("wsl_cdA", (128, K * 128))
    wsl_cdB = sb("wsl_cdB", (128, K * 128))
    dtr_s = sb("dtr_s", (R, TC))
    fix1 = sb("fix1", (128, S), F32)
    fix2 = sb("fix2", (128, S), F32)
    dtall = sb("dtall", (128, NDB * TC))
    dtsl = []
    for _i in range(NDB):
        bufs[f"dtall_{_i}"] = Buf(f"dtall_{_i}")
        dtsl.append(bufs[f"dtall_{_i}"])
    usl = []
    for _i in range(NDB):
        bufs[f"ubuf_{_i}"] = Buf(f"ubuf_{_i}")
        usl.append(bufs[f"ubuf_{_i}"])
    bufs["ubufdma"] = Buf("ubufdma")
    dBus = [dBuA, ytmpall]
    identf = sb("identf", (128, 128), F32)

    def psum(name, shape, dtype=F32):
        t = nc.alloc_psum_tensor(name, list(shape), dtype)
        bufs[name] = Buf(name)
        return t.ap()

    pAs = [psum("pA0", (128, TC)), psum("pA1", (128, TC))]
    pBs = [psum("pB0", (128, TC)), psum("pB1", (128, TC))]
    pYs = [psum("pY0", (128, TCS)), psum("pY1", (128, TCS))]
    pT = psum("pT", (128, 256))
    pStat = psum("pStat", (1, 512))

    Bf = lambda n: bufs[n]
    gtmp = Bbc.bitcast(F32)[:, 0:3072]
    mergeb_bc = Cbc.bitcast(F32)[:, 0:1024]
    ffnb_bc = Cbc.bitcast(F32)[:, 1024:2048]

    def mmsplit(mk, nk, ncols, reads, writes):
        fns = []
        for c0 in range(0, ncols, 512):
            c1 = min(ncols, c0 + 512)
            for kc in range(nk):
                fns.append(mk(kc, c0, c1, kc == 0, kc == nk - 1))
        P.group("tensor", fns, reads=reads, writes=writes)

    def dma(queue, dst_ap, src_ap, writes, reads):
        return P.dmaop(queue, lambda e: e.dma_start(out=dst_ap, in_=src_ap),
                       reads=list(reads), writes=list(writes))

    dma("gpsimd", ident_b[:, :], h["ident"][:, :], [Bf("ident_b")], [])
    dma("sync", identf[:, :], h["ident"][:, :], [Bf("identf")], [])
    dma("gpsimd", ones_b[:, :], h["onesv"][:, :], [Bf("ones_b")], [])
    dma("sync", masks_s[:, :], h["masks"][:, :], [Bf("masks_s")], [])
    constv_s = sb("constv_s", (128, 2), F32)
    dma("sync", constv_s[:, :], h["constv"][:, :], [Bf("constv_s")], [])
    nc.const_aps.register(F32, 1e-5, constv_s[:, 0:1])
    nc.const_aps.register(F32, -0.5, constv_s[:, 1:2])
    for _e in ENGINES:
        P._wait(_e, "b:constv_s", P.count.get("b:constv_s", 0))
    dma("sync", dtb_s.rearrange("p (l n) -> p l n", l=2 * NL), h["dtb"].rearrange("l p n -> p l n"), [Bf("dtb_s")], [])
    dma("sync", convb_s.rearrange("p (l n) -> p l n", l=2 * NL), h["convb"].rearrange("l p n -> p l n"), [Bf("convb_s")], [])
    dma("sync", Dp_s.rearrange("p (l n) -> p l n", l=2 * NL), h["Dpv"].rearrange("l p n -> p l n"), [Bf("Dp_s")], [])
    dma("sync", nw_s.rearrange("p (l n) -> p l n", l=2 * NL), h["nwv"].rearrange("l p n -> p l n"), [Bf("nw_s")], [])
    dma("sync", nb_s.rearrange("p (l n) -> p l n", l=2 * NL), h["nbv"].rearrange("l p n -> p l n"), [Bf("nb_s")], [])
    dma("sync", nfw_s[:, :], h["normfw"][:, :], [Bf("nfw_s")], [])
    dma("sync", nfb_s[:, :], h["normfb"][:, :], [Bf("nfb_s")], [])

    dma("sync", rs_s[:, :], h["rowscale"][:, :], [Bf("rs_s")], [])
    dma("sync", idx0_s[:, :], h["scatidx0"][:, :], [Bf("idx0_s")], [])
    dma("sync", idx1_s[:, :], h["scatidx1"][:, :], [Bf("idx1_s")], [])


    def zero_carries():
        P.op("vector", lambda e: e.memset(hcarry[:, :], 0.0), writes=[Bf("hcarry")])
        P.op("vector", lambda e: e.memset(utail[:, :], 0.0), writes=[Bf("utail")])
        P.op("vector", lambda e: e.memset(fix1[:, :], 0.0), writes=[Bf("fix1")])

    def cast_weight(src_ap, dst_ap, rows, cols, dstbuf):
        for r0 in range(0, rows, 128):
            rr = min(128, rows - r0)
            for c0 in range(0, cols, 2048):
                cc = min(2048, cols - c0)
                dma("sync", wstage2[:rr, :cc], src_ap[r0:r0 + rr, c0:c0 + cc],
                    [Bf("ytmpall")], [])
                P.op("vector", lambda e, rr=rr, cc=cc: e.tensor_copy(
                    out=wstage[:rr, :cc], in_=wstage2[:rr, :cc]),
                    reads=[Bf("ytmpall")], writes=[Bf("ytmpall")])
                dma("gpsimd", dst_ap[r0:r0 + rr, c0:c0 + cc], wstage[:rr, :cc],
                    [dstbuf], [Bf("ytmpall")])

    for ld in range(2 * NL):
        cast_weight(h["inWT"][ld], winWT[ld], D, 2 * DIN, dbuf("winWT"))
        cast_weight(h["outWT"][ld], woutWT[ld], DIN, D, dbuf("woutWT"))
        cast_weight(h["xpWTp"][ld], wxpWT[ld], DIN, 128, dbuf("wxpWT"))
        cast_weight(h["dtWT"][ld], wdtWT[ld], R, DIN, dbuf("wdtWT"))
    cast_weight(h["mergeWT"], wmergeT, 2 * D, D, dbuf("wmergeT"))
    for ee in range(EXPN):
        cast_weight(h["w1s"][ee], w1b[ee], D, FF, dbuf("w1b"))
        cast_weight(h["w2s"][ee], w2b[ee], FF, D, dbuf("w2b"))

    for ld in range(2 * NL):
        dma("sync", wstage2[:, :NDB * K], h["convw"][ld][:, :], [Bf("ytmpall")], [])
        for col in range(NDB * K):
            P.op("vector", lambda e, col=col: e.tensor_tensor(
                out=wstage[:, :128], in0=ident_b[:, :],
                in1=wstage2[:, col:col + 1].broadcast_to((128, 128)), op=ALU.mult),
                reads=[Bf("ytmpall"), Bf("ident_b")], writes=[Bf("ytmpall")])
            dma("gpsimd", wconvd[ld][col][:, :], wstage[:, :128], [dbuf("wconvd")], [Bf("ytmpall")])

    lnctr = [0]

    def layer_norm(src_fn, src_buf, dst_fn, dst_buf, gamma_fn, beta_fn, eps=1e-5):
        for c0 in range(0, TC, 512):
            c1 = min(TC, c0 + 512)
            fns = [(lambda e, dc=dc, c0=c0, c1=c1: e.matmul(
                out=pStat[:, 0:c1 - c0], lhsT=ones_b[:, :], rhs=src_fn(dc)[:, c0:c1],
                start=(dc == 0), stop=(dc == NDC - 1))) for dc in range(NDC)]
            P.group("tensor", fns, reads=[src_buf, Bf("ones_b")], writes=[Bf("pStat")])
            P.op("vector", lambda e, c0=c0, c1=c1: e.tensor_scalar_mul(
                out=mrow[:, c0:c1], in0=pStat[:, 0:c1 - c0], scalar1=1.0 / D),
                reads=[Bf("pStat")], writes=[Bf("mrow")])
        for c0 in range(0, TC, 512):
            c1 = min(TC, c0 + 512)
            fns = []
            for dc in range(NDC):
                P.op("scalar", lambda e, dc=dc, c0=c0, c1=c1: e.activation(
                    out=sqacc[:, c0:c1], in_=src_fn(dc)[:, c0:c1], func=AF.Square),
                    reads=[src_buf], writes=[Bf("sqacc")])
                P.op("tensor", lambda e, dc=dc, c0=c0, c1=c1: e.matmul(
                    out=pStat[:, 0:c1 - c0], lhsT=ones_b[:, :], rhs=sqacc[:, c0:c1],
                    start=(dc == 0), stop=(dc == NDC - 1)),
                    reads=[Bf("sqacc"), Bf("ones_b")], writes=[Bf("pStat")])
            P.op("vector", lambda e, c0=c0, c1=c1: e.tensor_scalar_mul(
                out=vrow[:, c0:c1], in0=pStat[:, 0:c1 - c0], scalar1=1.0 / D),
                reads=[Bf("pStat")], writes=[Bf("vrow")])
        P.op("vector", lambda e: e.tensor_tensor(
            out=etmp[0:1, :], in0=mrow[:, :], in1=mrow[:, :], op=ALU.mult),
            reads=[Bf("mrow")], writes=[Bf("etmp0")])
        P.op("vector", lambda e: e.tensor_tensor(
            out=vrow[:, :], in0=vrow[:, :], in1=etmp[0:1, :], op=ALU.subtract),
            reads=[Bf("etmp0")], writes=[Bf("vrow")])
        P.op("scalar", lambda e: e.activation(
            out=vrow[:, :], in_=vrow[:, :], func=AF.Ln, bias=eps),
            reads=[], writes=[Bf("vrow")])
        P.op("scalar", lambda e: e.activation(
            out=vrow[:, :], in_=vrow[:, :], func=AF.Exp, scale=-0.5),
            reads=[], writes=[Bf("vrow")])
        sl = lnctr[0] % 2
        lnctr[0] += 1
        dma("gpsimd", statrow[2 * sl:2 * sl + 1, :], mrow[:, :], [], [Bf("mrow")])
        dma("gpsimd", statrow[2 * sl + 1:2 * sl + 2, :], vrow[:, :], [], [Bf("vrow")])
        dma("gpsimd", mubc[:, :], statrow[2 * sl:2 * sl + 1, :].broadcast_to((128, TC)),
            [Bf("mubc")], [])
        dma("gpsimd", rsbc[:, :], statrow[2 * sl + 1:2 * sl + 2, :].broadcast_to((128, TC)),
            [Bf("rsbc")], [])
        for dc in range(NDC):
            P.op("vector", lambda e, dc=dc: e.tensor_tensor(
                out=dst_fn(dc), in0=src_fn(dc), in1=mubc[:, :], op=ALU.subtract),
                reads=[src_buf, Bf("mubc")], writes=[dst_buf])
            P.op("vector", lambda e, dc=dc: e.tensor_tensor(
                out=dst_fn(dc), in0=dst_fn(dc), in1=rsbc[:, :], op=ALU.mult),
                reads=[Bf("rsbc")], writes=[dst_buf])
            if gamma_fn is not None:
                P.op("vector", lambda e, dc=dc: e.tensor_scalar(
                    out=dst_fn(dc), in0=dst_fn(dc),
                    scalar1=gamma_fn(dc), scalar2=beta_fn(dc),
                    op0=ALU.mult, op1=ALU.add),
                    reads=[Bf("nw_s"), Bf("nb_s"), Bf("nfw_s"), Bf("nfb_s")],
                    writes=[dst_buf])

    rview = lambda dc: rbuf[:, dc * TC:(dc + 1) * TC]
    xview = lambda dc: xhat[:, dc * TC:(dc + 1) * TC]

    def run_stack(dname, x_ap, stash_ap, killcol):
        stash_name = dname + "stash"
        zero_carries()
        for c in range(NCHUNK):
            warm = (c < NWARM)
            t0 = c * TC
            for l in range(NL):
                ld = (0 if dname == "fw" else NL) + l
                dma("sync", Aneg[:, :], h["Alogp"][ld:ld + 1, :, :].rearrange("l p n -> p (l n)"),
                    [Bf("Aneg")], [])
                P.op("scalar", lambda e: e.activation(out=Aneg[:, :], in_=Aneg[:, :],
                                                      func=AF.Exp),
                     reads=[], writes=[Bf("Aneg")])
                P.op("vector", lambda e: e.tensor_scalar_mul(out=Aneg[:, :], in0=Aneg[:, :],
                                                             scalar1=-1.0),
                     reads=[], writes=[Bf("Aneg")])
                if l == 0:
                    for dc in range(NDC):
                        dma("gpsimd", rview(dc),
                            x_ap[dc * 128:(dc + 1) * 128, t0:t0 + TC], [Bf("rbuf")], [])
                    layer_norm(rview, Bf("rbuf"), rview, Bf("rbuf"), None, None)
                layer_norm(rview, Bf("rbuf"), xview, Bf("xhat"),
                           (lambda dc, ld=ld: nw_s[:, ld * NDC + dc:ld * NDC + dc + 1]),
                           (lambda dc, ld=ld: nb_s[:, ld * NDC + dc:ld * NDC + dc + 1]))

                wsrc = winWT[ld].rearrange("(kc p) f -> p kc f", p=128)
                for f in range(NDB):
                    wv = [wsl_inA, wsl_inB][f % 2]
                    wvb = Bf(["wsl_inA", "wsl_inB"][f % 2])
                    pA = pAs[f % 2]
                    pAb = Bf(["pA0", "pA1"][f % 2])
                    dma("sync", wv.rearrange("p (kc f) -> p kc f", kc=NDC),
                        wsrc[:, :, f * 128:(f + 1) * 128], [wvb], [dbuf("winWT")])
                    mmsplit(lambda kc, c0, c1, st, sp, wv=wv, pA=pA: (
                        lambda e: e.matmul(
                            out=pA[:, c0:c1], lhsT=wv[:, kc * 128:(kc + 1) * 128],
                            rhs=xview(kc)[:, c0:c1], start=st, stop=sp)),
                        NDC, TC, [wvb, Bf("xhat")], [pAb])
                    if warm:
                        P.op("vector", lambda e, f=f, pA=pA: e.tensor_scalar_mul(
                            out=ubuf[:, f * TC:(f + 1) * TC], in0=pA[:, :],
                            scalar1=masks_s[:, killcol:killcol + 1]),
                            reads=[pAb, Bf("masks_s")], writes=[usl[f], Bf("ubufdma")])
                    else:
                        P.op("vector", lambda e, f=f, pA=pA: e.tensor_copy(
                            out=ubuf[:, f * TC:(f + 1) * TC], in_=pA[:, :]),
                            reads=[pAb], writes=[usl[f], Bf("ubufdma")])

                cdsrc = wconvd[ld].rearrange("n q r -> q n r")
                for db in range(NDB):
                    cdv = [wsl_cdA, wsl_cdB][db % 2]
                    cdvb = Bf(["wsl_cdA", "wsl_cdB"][db % 2])
                    pB = pBs[db % 2]
                    pBb = Bf(["pB0", "pB1"][db % 2])
                    dma("sync", cdv.rearrange("q (n r) -> q n r", n=K),
                        cdsrc[:, db * K:(db + 1) * K, :], [cdvb], [dbuf("wconvd")])
                    fns = []
                    for c0 in range(0, TC, 512):
                        c1 = c0 + 512
                        specs = []
                        for k in range(K):
                            sh = k - (K - 1)
                            lo = max(-sh, c0)
                            specs.append(("main", k, sh, lo))
                            if c0 == 0 and -sh > 0:
                                specs.append(("tail", k, sh, -sh))
                        for i, (kind, k, sh, lo) in enumerate(specs):
                            first, last = (i == 0), (i == len(specs) - 1)
                            if kind == "main":
                                fns.append((lambda e, db=db, k=k, sh=sh, lo=lo, c1=c1,
                                            cdv=cdv, pB=pB, first=first, last=last: e.matmul(
                                    out=pB[:, lo:c1], lhsT=cdv[:, k * 128:(k + 1) * 128],
                                    rhs=ubuf[:, db * TC + lo + sh: db * TC + c1 + sh],
                                    start=first, stop=last)))
                            else:
                                fns.append((lambda e, db=db, k=k, lo=lo, cdv=cdv, l=l,
                                            pB=pB, first=first, last=last: e.matmul(
                                    out=pB[:, 0:lo], lhsT=cdv[:, k * 128:(k + 1) * 128],
                                    rhs=utail[:, (l * NDB + db) * (K - 1) + k:
                                               (l * NDB + db) * (K - 1) + k + lo],
                                    start=first, stop=last)))
                    P.group("tensor", fns,
                            reads=[cdvb, usl[db], Bf("utail")], writes=[pBb])
                    P.op("vector", lambda e, db=db, l=l: e.tensor_copy(
                        out=utail[:, (l * NDB + db) * (K - 1):(l * NDB + db) * (K - 1) + K - 1],
                        in_=ubuf[:, db * TC + TC - (K - 1): db * TC + TC]),
                        reads=[usl[db]], writes=[Bf("utail")])
                    P.op("scalar", lambda e, db=db, ld=ld, pB=pB: e.activation(
                        out=ubuf[:, db * TC:(db + 1) * TC], in_=pB[:, :], func=AF.Silu,
                        bias=convb_s[:, ld * NDB + db: ld * NDB + db + 1]),
                        reads=[pBb, Bf("convb_s"), Bf("utail")], writes=[usl[db]])

                dma("sync", wsl_xp.rearrange("p (kc f) -> p kc f", kc=NDB),
                    wxpWT[ld].rearrange("(kc p) f -> p kc f", p=128), [Bf("wsl_xp")],
                    [dbuf("wxpWT")])
                pA = pAs[0]
                pAb = Bf("pA0")
                mmsplit(lambda kc, c0, c1, st, sp, pA=pA: (
                    lambda e: e.matmul(
                        out=pA[:, c0:c1], lhsT=wsl_xp[:, kc * 128:(kc + 1) * 128],
                        rhs=ubuf[:, kc * TC + c0:kc * TC + c1], start=st, stop=sp)),
                    NDB, TC, [Bf("wsl_xp")] + usl, [pAb])
                P.op("vector", lambda e, pA=pA: e.tensor_copy(out=dtr_s[:, :], in_=pA[64:128, :]),
                     reads=[pAb], writes=[Bf("dtr_s")])
                slbc = c % 2
                P.op("vector", lambda e, pA=pA: e.tensor_copy(out=sqacc[0:16, :], in_=pA[0:16, :]),
                     reads=[pAb], writes=[Bf("sqacc")])
                P.op("vector", lambda e, pA=pA: e.tensor_copy(out=sqacc[32:48, :], in_=pA[32:48, :]),
                     reads=[pAb], writes=[Bf("sqacc")])
                bv = bcrow[slbc:slbc + 1, :]
                dma("gpsimd", bv[:, 0:16 * TC].rearrange("o (s t) -> (o s) t", s=16),
                    sqacc[0:16, :], [], [Bf("sqacc")])
                dma("gpsimd", bv[:, 16 * TC:32 * TC].rearrange("o (s t) -> (o s) t", s=16),
                    sqacc[32:48, :], [], [Bf("sqacc")])
                for m in range(NSUB):
                    dma("gpsimd",
                        Bbc[:, m * S * TCS:(m + 1) * S * TCS]
                        .rearrange("p (s t) -> p s t", s=S),
                        bv[:, 0:16 * TC].rearrange("o (s m t) -> o m s t", s=S, m=NSUB)
                        [:, m, :, :].broadcast_to((128, S, TCS)), [Bf("Bbc")], [])
                    dma("gpsimd",
                        Cbc[:, m * S * TCS:(m + 1) * S * TCS]
                        .rearrange("p (s t) -> p s t", s=S),
                        bv[:, 16 * TC:32 * TC].rearrange("o (s m t) -> o m s t", s=S, m=NSUB)
                        [:, m, :, :].broadcast_to((128, S, TCS)), [Bf("Cbc")], [])
                dma("sync", wsl_dt[:, :], wdtWT[ld][:, :], [Bf("wsl_dt")], [dbuf("wdtWT")])

                hbf3 = hbf.rearrange("p (s t) -> p s t", s=S)
                for db in range(NDB):
                    par = db % 2
                    pB = pBs[par]
                    pBb = Bf(["pB0", "pB1"][par])
                    etmp = etmps[par]
                    etmpb = Bf(["etmp0", "etmp1"][par])
                    mmsplit(lambda kc, c0, c1, st, sp, db=db, pB=pB: (
                        lambda e: e.matmul(
                            out=pB[:, c0:c1], lhsT=wsl_dt[:, db * 128:(db + 1) * 128],
                            rhs=dtr_s[:, c0:c1], start=True, stop=True)),
                        1, TC, [Bf("wsl_dt"), Bf("dtr_s")], [pBb])
                    P.op("scalar", lambda e, db=db, ld=ld, pB=pB, etmp=etmp: e.activation(
                        out=etmp[:, :], in_=pB[:, :], func=AF.Exp,
                        bias=dtb_s[:, ld * NDB + db:ld * NDB + db + 1]),
                        reads=[pBb, Bf("dtb_s")], writes=[etmpb])
                    P.op("scalar", lambda e, db=db, etmp=etmp: e.activation(
                        out=dtall[:, db * TC:(db + 1) * TC], in_=etmp[:, :],
                        func=AF.Ln, bias=1.0),
                        reads=[etmpb], writes=[dtsl[db]])
                def emit_stt(db2):
                    par2 = db2 % 2
                    pY2 = pYs[par2]
                    pYb2 = Bf(["pY0", "pY1"][par2])
                    P.op("vector", lambda e, db=db2, ld=ld, pY=pY2: e.scalar_tensor_tensor(
                        out=ubuf[:, db * TC: (db + 1) * TC],
                        in0=ubuf[:, db * TC: (db + 1) * TC],
                        scalar=Dp_s[:, ld * NDB + db:ld * NDB + db + 1],
                        in1=pY[:, :], op0=ALU.mult, op1=ALU.add),
                        reads=[usl[db2], Bf("Dp_s"), pYb2], writes=[usl[db2]])
                for db in range(NDB):
                    par = db % 2
                    pY = pYs[par]
                    pYb = Bf(["pY0", "pY1"][par])
                    dAb = [dAb0, dAb1][par]
                    dAbb = Bf(["dAb0", "dAb1"][par])
                    dtublk = dtublks[par]
                    dtublkb = Bf(["dtublk0", "dtublk1"][par])
                    dbu = dBus[par]
                    dbub = Bf(["dBuA", "ytmpall"][par])
                    dbu3 = dbu.rearrange("p (s t) -> p s t", s=S)
                    dA4 = dAb.rearrange("p (m s t) -> p m s t", m=NSUB, s=S)
                    dt4 = dtall[:, db * TC:(db + 1) * TC].rearrange("p (m t) -> p m t", m=NSUB)
                    P.op("gpsimd", lambda e, db=db, dtublk=dtublk: e.tensor_tensor(
                        out=dtublk[:, :], in0=dtall[:, db * TC:(db + 1) * TC],
                        in1=ubuf[:, db * TC:(db + 1) * TC], op=ALU.mult),
                        reads=[dtsl[db], usl[db]], writes=[dtublkb])
                    if warm:
                        P.op("gpsimd", lambda e, dtublk=dtublk: e.tensor_scalar_mul(
                            out=dtublk[:, :], in0=dtublk[:, :],
                            scalar1=masks_s[:, killcol:killcol + 1]),
                            reads=[Bf("masks_s")], writes=[dtublkb])
                    for s_ in range(S):
                        P.op("scalar", lambda e, s_=s_, db=db, dA4=dA4, dt4=dt4: e.activation(
                            out=dA4[:, :, s_, :], in_=dt4[:, :, :], func=AF.Exp,
                            scale=Aneg[:, db * S + s_: db * S + s_ + 1]),
                            reads=[dtsl[db], Bf("Aneg")], writes=[dAbb])
                    hc = hcarry[:, (l * NDB + db) * S:(l * NDB + db + 1) * S]
                    for m in range(NSUB):
                        dAm = dAb[:, m * S * TCS:(m + 1) * S * TCS]
                        dAm3 = dAm.rearrange("p (s t) -> p s t", s=S)
                        PS = 8
                        P.op("gpsimd", lambda e, m=m, dtublk=dtublk, dbu3=dbu3, PS=PS:
                            e.tensor_tensor(
                            out=dbu3[:, 0:PS, :],
                            in0=dtublk[:, m * TCS:(m + 1) * TCS]
                            .rearrange("p (o t) -> p o t", o=1).broadcast_to((128, PS, TCS)),
                            in1=Bbc[:, m * S * TCS:m * S * TCS + PS * TCS]
                            .rearrange("p (s t) -> p s t", s=PS),
                            op=ALU.mult),
                            reads=[dtublkb, Bf("Bbc")], writes=[dbub])
                        P.op("vector", lambda e, m=m, dtublk=dtublk, dbu3=dbu3, PS=PS:
                            e.tensor_tensor(
                            out=dbu3[:, PS:S, :],
                            in0=dtublk[:, m * TCS:(m + 1) * TCS]
                            .rearrange("p (o t) -> p o t", o=1).broadcast_to((128, S - PS, TCS)),
                            in1=Bbc[:, m * S * TCS + PS * TCS:(m + 1) * S * TCS]
                            .rearrange("p (s t) -> p s t", s=S - PS),
                            op=ALU.mult),
                            reads=[dtublkb, Bf("Bbc")], writes=[dbub])
                        P.op("vector", lambda e, dAm3=dAm3, hc=hc: e.scalar_tensor_tensor(
                            out=fix2[:, 1:S], in0=dAm3[:, 1:S, 0], scalar=1.0,
                            in1=hc[:, 1:S], op0=ALU.mult, op1=ALU.mult),
                            reads=[dAbb, Bf("hcarry")], writes=[Bf("fix2")])
                        P.op("vector", lambda e, dbu3=dbu3: e.scalar_tensor_tensor(
                            out=dbu3[:, 1:S, 0], in0=dbu3[:, 1:S, 0], scalar=1.0,
                            in1=fix2[:, 1:S], op0=ALU.mult, op1=ALU.add),
                            reads=[Bf("fix2"), dbub], writes=[dbub])
                        P.op("vector", lambda e, dAm3=dAm3: e.memset(dAm3[:, 1:S, 0], 0.0),
                             reads=[], writes=[dAbb])
                        P.op("vector", lambda e, dAm=dAm, dbu=dbu, hc=hc: e.tensor_tensor_scan(
                            out=hbf[:, :], data0=dAm, data1=dbu[:, :],
                            initial=hc[:, 0:1], op0=ALU.mult, op1=ALU.add),
                            reads=[dAbb, dbub, Bf("hcarry")], writes=[Bf("hbf")])
                        P.op("vector", lambda e, hc=hc: e.scalar_tensor_tensor(
                            out=hc[:, :], in0=hbf3[:, :, TCS - 1], scalar=1.0,
                            in1=fix1[:, :], op0=ALU.mult, op1=ALU.add),
                            reads=[Bf("hbf"), Bf("fix1")], writes=[Bf("hcarry")])
                        P.op("vector", lambda e, m=m, dbu=dbu: e.tensor_tensor(
                            out=dbu[:, :], in0=hbf[:, :],
                            in1=Cbc[:, m * S * TCS:(m + 1) * S * TCS], op=ALU.mult),
                            reads=[Bf("hbf"), Bf("Cbc")], writes=[dbub])
                        fns = [(lambda e, s_=s_, pY=pY, dbu=dbu: e.matmul(
                            out=pY[:, :], lhsT=ident_b[:, :],
                            rhs=dbu[:, s_ * TCS:(s_ + 1) * TCS],
                            start=(s_ == 0), stop=(s_ == S - 1))) for s_ in range(S)]
                        P.group("tensor", fns, reads=[dbub, Bf("ident_b")],
                                writes=[pYb])
                    if db > 0:
                        emit_stt(db - 1)
                emit_stt(NDB - 1)

                for db in range(NDB):
                    par = db % 2
                    pB = pBs[par]
                    pBb = Bf(["pB0", "pB1"][par])
                    siluz = siluzs[par]
                    siluzb = Bf(["siluz0", "siluz1"][par])
                    f = NDB + db
                    wv = [wsl_inA, wsl_inB][par]
                    wvb = Bf(["wsl_inA", "wsl_inB"][par])
                    dma("sync", wv.rearrange("p (kc f) -> p kc f", kc=NDC),
                        wsrc[:, :, f * 128:(f + 1) * 128], [wvb], [dbuf("winWT")])
                    mmsplit(lambda kc, c0, c1, st, sp, wv=wv, pB=pB: (
                        lambda e: e.matmul(
                            out=pB[:, c0:c1], lhsT=wv[:, kc * 128:(kc + 1) * 128],
                            rhs=xview(kc)[:, c0:c1], start=st, stop=sp)),
                        NDC, TC, [wvb, Bf("xhat")], [pBb])
                    P.op("scalar", lambda e, pB=pB, siluz=siluz: e.activation(
                        out=siluz[:, :], in_=pB[:, :], func=AF.Silu),
                        reads=[pBb], writes=[siluzb])
                    P.op("vector", lambda e, db=db, siluz=siluz: e.tensor_tensor(
                        out=ubuf[:, db * TC:(db + 1) * TC],
                        in0=ubuf[:, db * TC:(db + 1) * TC], in1=siluz[:, :],
                        op=ALU.mult),
                        reads=[usl[db], siluzb], writes=[usl[db]])

                osrc = woutWT[ld].rearrange("(kc p) f -> p kc f", p=128)
                for f in range(NDC):
                    wv = [wsl_outA, wsl_outB][f % 2]
                    wvb = Bf(["wsl_outA", "wsl_outB"][f % 2])
                    pA = pAs[f % 2]
                    pAb = Bf(["pA0", "pA1"][f % 2])
                    dma("sync", wv.rearrange("p (kc f) -> p kc f", kc=16),
                        osrc[:, :, f * 128:(f + 1) * 128], [wvb], [dbuf("woutWT")])
                    mmsplit(lambda kc, c0, c1, st, sp, wv=wv, pA=pA: (
                        lambda e: e.matmul(
                            out=pA[:, c0:c1], lhsT=wv[:, kc * 128:(kc + 1) * 128],
                            rhs=ubuf[:, kc * TC + c0:kc * TC + c1], start=st, stop=sp)),
                        NDB, TC, [wvb] + usl, [pAb])
                    if l < NL - 1:
                        P.op("vector", lambda e, f=f, pA=pA: e.tensor_tensor(
                            out=rview(f), in0=rview(f), in1=pA[:, :], op=ALU.add),
                            reads=[pAb], writes=[Bf("rbuf")])
                    else:
                        P.op("vector", lambda e, f=f, pA=pA: e.tensor_tensor(
                            out=xview(f), in0=rview(f), in1=pA[:, :], op=ALU.add),
                            reads=[pAb, Bf("rbuf")], writes=[Bf("xhat")])
                if l == NL - 1 and not warm:
                    layer_norm(xview, Bf("xhat"), rview, Bf("rbuf"),
                               (lambda dc: nfw_s[:, dc:dc + 1]),
                               (lambda dc: nfb_s[:, dc:dc + 1]))
                    for dc in range(NDC):
                        dma("gpsimd",
                            stash_ap[dc * 128:(dc + 1) * 128,
                                     (c - NWARM) * TC:(c - NWARM + 1) * TC],
                            rview(dc), [dbuf(stash_name)], [Bf("rbuf")])

    run_stack("fw", h["xfwT"], fwstash, killcol=0)
    run_stack("bw", h["xbwT"], bwstash, killcol=1)

    for rc in range(NROW // TC):
        for dc in range(NDC):
            dma("gpsimd", rview(dc),
                h["xrowsT"][dc * 128:(dc + 1) * 128, rc * TC:(rc + 1) * TC], [Bf("rbuf")], [])
        layer_norm(rview, Bf("rbuf"), rview, Bf("rbuf"), None, None)
        for dc in range(NDC):
            dma("gpsimd", xnrows[dc * 128:(dc + 1) * 128, rc * TC:(rc + 1) * TC],
                rview(dc), [dbuf("xnrows")], [Bf("rbuf")])

    for (ee, r0, r1) in moe_segs:
        n = r1 - r0
        for kc in range(NDC):
            dma("sync", xhat[:, kc * TC:kc * TC + n],
                xnrows[kc * 128:(kc + 1) * 128, r0:r1], [Bf("xhat")], [dbuf("xnrows")])
        w1src = w1b[ee].rearrange("(kc p) f -> p kc f", p=128)
        for f in range(FF // 128):
            wv = [wsl_inA, wsl_inB][f % 2]
            wvb = Bf(["wsl_inA", "wsl_inB"][f % 2])
            pA = pAs[f % 2]
            pAb = Bf(["pA0", "pA1"][f % 2])
            dma("sync", wv.rearrange("p (kc f) -> p kc f", kc=NDC),
                w1src[:, :, f * 128:(f + 1) * 128], [wvb], [dbuf("w1b")])
            fns = [(lambda e, kc=kc, wv=wv, n=n, pA=pA: e.matmul(
                out=pA[:, :n], lhsT=wv[:, kc * 128:(kc + 1) * 128],
                rhs=xhat[:, kc * TC:kc * TC + n],
                start=(kc == 0), stop=(kc == NDC - 1))) for kc in range(NDC)]
            P.group("tensor", fns, reads=[wvb, Bf("xhat")], writes=[pAb])
            P.op("scalar", lambda e, n=n, pA=pA: e.activation(
                out=sqacc[:, :n], in_=pA[:, :n], func=AF.Gelu),
                reads=[pAb], writes=[Bf("sqacc")])
            dma("gpsimd", hrows[f * 128:(f + 1) * 128, r0:r1], sqacc[:, :n],
                [dbuf("hrows")], [Bf("sqacc")])

    for (ee, r0, r1) in moe_segs:
        n = r1 - r0
        for kc in range(32):
            if kc < 16:
                dst = ubuf[:, kc * TC:kc * TC + n]
            else:
                dst = dAb0[:, (kc - 16) * 512:(kc - 16) * 512 + n]
            dma("sync", dst, hrows[kc * 128:(kc + 1) * 128, r0:r1],
                ([Bf("ubufdma"), usl[kc]] if kc < 16 else [Bf("dAb0")]),
                [dbuf("hrows")])
        w2src = w2b[ee].rearrange("(kc p) f -> p kc f", p=128)
        for f in range(NDC):
            pB = pBs[f % 2]
            pBb = Bf(["pB0", "pB1"][f % 2])
            for half in range(2):
                wv = [wsl_outA, wsl_outB][half]
                wvb = Bf(["wsl_outA", "wsl_outB"][half])
                dma("sync", wv.rearrange("p (kc f) -> p kc f", kc=16),
                    w2src[:, half * 16:half * 16 + 16, f * 128:(f + 1) * 128],
                    [wvb], [dbuf("w2b")])
            fns = []
            for kc in range(32):
                wv = [wsl_outA, wsl_outB][kc // 16]
                if kc < 16:
                    mv = ubuf[:, kc * TC:kc * TC + n]
                else:
                    mv = dAb0[:, (kc - 16) * 512:(kc - 16) * 512 + n]
                fns.append((lambda e, kc=kc, wv=wv, mv=mv, n=n, pB=pB: e.matmul(
                    out=pB[:, :n], lhsT=wv[:, (kc % 16) * 128:((kc % 16) + 1) * 128],
                    rhs=mv, start=(kc == 0), stop=(kc == 31))))
            P.group("tensor", fns,
                    reads=[Bf("wsl_outA"), Bf("wsl_outB"), Bf("ubufdma"), Bf("dAb0")],
                    writes=[pBb])
            P.op("vector", lambda e, n=n, pB=pB: e.tensor_copy(out=wstage2[:, :n], in_=pB[:, :n]),
                 reads=[pBb], writes=[Bf("ytmpall")])
            dma("gpsimd", moeFM[f * 128:(f + 1) * 128, r0:r1], wstage2[:, :n],
                [dbuf("moeFM")], [Bf("ytmpall")])

    for rb in range(NROW // 128):
        for f in range(NDC):
            dma("sync", wstage2[:, f * 128:(f + 1) * 128],
                moeFM[f * 128:(f + 1) * 128, rb * 128:(rb + 1) * 128], [Bf("ytmpall")],
                [dbuf("moeFM")])
            P.op("tensor", lambda e, f=f: e.transpose(
                out=pT[:, (f % 2) * 128:(f % 2) * 128 + 128],
                in_=wstage2[:, f * 128:(f + 1) * 128], identity=identf[:, :]),
                reads=[Bf("ytmpall"), Bf("identf")], writes=[Bf("pT")])
            P.op("vector", lambda e, f=f, rb=rb: e.tensor_scalar_mul(
                out=gtmp[:, f * 128:(f + 1) * 128],
                in0=pT[:, (f % 2) * 128:(f % 2) * 128 + 128],
                scalar1=rs_s[:, rb:rb + 1]),
                reads=[Bf("pT"), Bf("rs_s")], writes=[Bf("Bbc")])
        P.dmaop("gpsimd", lambda e, rb=rb: e.indirect_dma_start(
            out=gscat0[:, :], out_offset=IndirectOffsetOnAxis(ap=idx0_s[:, rb:rb + 1], axis=0),
            in_=gtmp[:, 0:1024], in_offset=None),
            reads=[Bf("Bbc"), Bf("idx0_s")], writes=[dbuf("gscat")])
        P.dmaop("gpsimd", lambda e, rb=rb: e.indirect_dma_start(
            out=gscat1[:, :], out_offset=IndirectOffsetOnAxis(ap=idx1_s[:, rb:rb + 1], axis=0),
            in_=gtmp[:, 0:1024], in_offset=None),
            reads=[Bf("Bbc"), Bf("idx1_s")], writes=[dbuf("gscat")])

    gsbuf = dbuf("gscat")

    if DEBUG_STASH:
        for dc in range(NDC):
            for cc in range(2):
                dma("gpsimd", wstage2[:, 0:2048],
                    fwstash[dc * 128:(dc + 1) * 128, cc * 2048:(cc + 1) * 2048],
                    [Bf("ytmpall")], [])
                dma("gpsimd", dbg_fw[dc * 128:(dc + 1) * 128, cc * 2048:(cc + 1) * 2048],
                    wstage2[:, 0:2048], [], [Bf("ytmpall")])
                dma("gpsimd", wstage2[:, 0:2048],
                    bwstash[dc * 128:(dc + 1) * 128, cc * 2048:(cc + 1) * 2048],
                    [Bf("ytmpall")], [])
                dma("gpsimd", dbg_bw[dc * 128:(dc + 1) * 128, cc * 2048:(cc + 1) * 2048],
                    wstage2[:, 0:2048], [], [Bf("ytmpall")])
        for rr in range(CORE_T // 128):
            dma("gpsimd", wstage2[:, 0:1024], gscat0[rr * 128:(rr + 1) * 128, :],
                [Bf("ytmpall")], [gsbuf])
            dma("gpsimd", wstage2[:, 1024:2048], gscat1[rr * 128:(rr + 1) * 128, :],
                [Bf("ytmpall")], [gsbuf])
            P.op("vector", lambda e: e.tensor_tensor(
                out=wstage2[:, 0:1024], in0=wstage2[:, 0:1024], in1=wstage2[:, 1024:2048],
                op=ALU.add), reads=[], writes=[Bf("ytmpall")])
            dma("gpsimd", dbg_g[rr * 128:(rr + 1) * 128, :], wstage2[:, 0:1024],
                [], [Bf("ytmpall")])

    def mslot(kc):
        if kc < 8:
            return ubuf[:, kc * 1024:(kc + 1) * 1024]
        return dAb0[:, (kc - 8) * 1024:(kc - 7) * 1024]

    dma("sync", mergeb_bc, h["mergeb"].broadcast_to((128, D)), [Bf("Cbc")], [])
    dma("sync", ffnb_bc, h["ffnb"].broadcast_to((128, D)), [Bf("Cbc")], [])
    for kc in range(16):
        dma("sync", mslot(kc), wmergeT[kc * 128:(kc + 1) * 128, :],
            ([Bf("ubufdma"), usl[2 * kc], usl[2 * kc + 1]] if kc < 8
             else [Bf("dAb0")]), [dbuf("wmergeT")])
    for blk in range(CORE_T // 128):
        i0 = blk * 128
        for dc in range(NDC):
            dma("sync", wsl_inA[:, dc * 128:(dc + 1) * 128],
                fwstash[dc * 128:(dc + 1) * 128, i0:i0 + 128], [Bf("wsl_inA")],
                [dbuf("fwstash")])
        for dc in range(NDC):
            dma("sync", wsl_outA[:, dc * 128:(dc + 1) * 128],
                bwstash[dc * 128:(dc + 1) * 128, CORE_T - i0 - 128:CORE_T - i0],
                [Bf("wsl_outA")], [dbuf("bwstash")])
            P.op("vector", lambda e, dc=dc: e.tensor_copy(
                out=wsl_inB[:, dc * 128:(dc + 1) * 128],
                in_=wsl_outA[:, dc * 128:(dc + 1) * 128][:, ::-1]),
                reads=[Bf("wsl_outA")], writes=[Bf("wsl_inB")])

        def mlhs(kc):
            if kc < 8:
                return wsl_inA[:, kc * 128:(kc + 1) * 128]
            return wsl_inB[:, (kc - 8) * 128:(kc - 7) * 128]

        for hh in range(2):
            c0 = hh * 512
            pM = pAs[hh]
            pMb = Bf(["pA0", "pA1"][hh])
            fns = [(lambda e, kc=kc, c0=c0, pM=pM: e.matmul(
                out=pM[:, :], lhsT=mlhs(kc),
                rhs=mslot(kc)[:, c0:c0 + 512],
                start=(kc == 0), stop=(kc == 15))) for kc in range(16)]
            P.group("tensor", fns,
                    reads=[Bf("wsl_inA"), Bf("wsl_inB"), Bf("ubufdma"), Bf("dAb0")],
                    writes=[pMb])
        g0 = gtmp[:, 0:1024]
        g1 = gtmp[:, 1024:2048]
        gsum = gtmp[:, 2048:3072]
        gfin = gtmp[:, 2048:3072]
        sgp = gtmp[:, 1024:2048]
        dma("sync", g0, gscat0[i0:i0 + 128, :], [Bf("Bbc")], [gsbuf])
        dma("sync", g1, gscat1[i0:i0 + 128, :], [Bf("Bbc")], [gsbuf])
        P.op("vector", lambda e, g0=g0, g1=g1, gsum=gsum: e.tensor_tensor(
            out=gsum, in0=g0, in1=g1, op=ALU.add),
            reads=[Bf("Bbc")], writes=[Bf("Bbc")])
        P.op("vector", lambda e, gsum=gsum: e.scalar_tensor_tensor(
            out=gtmp[:, 0:1024], in0=gsum, scalar=masks_s[:, 2:3],
            in1=ffnb_bc[:, :], op0=ALU.mult, op1=ALU.add),
            reads=[Bf("masks_s"), Bf("Cbc")], writes=[Bf("Bbc")])
        P.op("scalar", lambda e, gfin=gfin: e.activation(
            out=gfin, in_=gtmp[:, 0:1024], func=AF.Silu),
            reads=[], writes=[Bf("Bbc")])
        P.op("vector", lambda e: e.tensor_tensor(
            out=wstage2[:, 0:512], in0=pAs[0][:, :], in1=mergeb_bc[:, 0:512], op=ALU.add),
            reads=[Bf("pA0"), Bf("Cbc")], writes=[Bf("ytmpall")])
        P.op("vector", lambda e: e.tensor_tensor(
            out=wstage2[:, 512:1024], in0=pAs[1][:, :], in1=mergeb_bc[:, 512:1024],
            op=ALU.add),
            reads=[Bf("pA1"), Bf("Cbc")], writes=[Bf("ytmpall")])
        P.op("vector", lambda e, sgp=sgp, gfin=gfin: e.tensor_tensor(
            out=sgp, in0=wstage2[:, 0:1024], in1=gfin, op=ALU.mult),
            reads=[Bf("ytmpall")], writes=[Bf("Bbc")])
        dma("sync", wstage2[:, 1024:2048], h["xown"][i0:i0 + 128, :], [Bf("ytmpall")], [])
        P.op("vector", lambda e, sgp=sgp: e.tensor_tensor(
            out=wstage2[:, 0:1024], in0=wstage2[:, 1024:2048], in1=sgp, op=ALU.add),
            reads=[Bf("Bbc")], writes=[Bf("ytmpall")])
        dma("gpsimd", out_h[i0:i0 + 128, :], wstage2[:, 0:1024], [], [Bf("ytmpall")])

    P._wait("gpsimd", "b:ytmpall", P.count.get("b:ytmpall", 0))

    import contextlib
    semnames = set(P.count.keys())
    for e in ENGINES:
        for it in P.items[e]:
            if it[0] == "wait":
                semnames.add(it[1])
    with contextlib.ExitStack() as stack:
        sems = {}
        for i, name in enumerate(sorted(semnames)):
            sems[name] = stack.enter_context(nc.semaphore(f"sm{i}"))
        block = stack.enter_context(nc.Block())

        waited = {}
        for e2 in ENGINES:
            for it in P.items[e2]:
                if it[0] == "wait":
                    waited.setdefault(it[1], set()).add(it[2])
        for s2, c2 in P.count.items():
            if c2 > 0:
                waited.setdefault(s2, set()).add(c2)

        def replay(engine, eng):
            running = {}
            pending = {}
            for item in P.items[engine]:
                if item[0] == "wait":
                    _, sem, val = item
                    eng.wait_ge(sems[sem], val)
                else:
                    _, fns, incv, semname = item
                    ins = None
                    for fn in fns:
                        ins = fn(eng)
                    if semname == engine:
                        cnt = running.get(semname, 0) + incv
                        running[semname] = cnt
                        pend = pending.get(semname, 0) + incv
                        if cnt in waited.get(semname, ()):
                            ins.then_inc(sems[semname], pend)
                            pending[semname] = 0
                        else:
                            pending[semname] = pend
                    else:
                        ins.then_inc(sems[semname], incv)

        @block.sync
        def _(eng):
            replay("sync", eng)

        @block.vector
        def _(eng):
            replay("vector", eng)

        @block.scalar
        def _(eng):
            replay("scalar", eng)

        @block.tensor
        def _(eng):
            replay("tensor", eng)

        @block.gpsimd
        def _(eng):
            replay("gpsimd", eng)

    return nc, P


def _host_prep(inputs):
    x = np.asarray(inputs["L0_attn_x"])
    maps = []

    def wstack(pref, name):
        return np.stack([np.asarray(inputs[f"L0_attn_{d}_{name}"][l])
                         for d in ("fw", "bw") for l in range(NL)], axis=0)

    inW = wstack("", "in_W")
    outW = wstack("", "out_W")
    xpW = wstack("", "xproj_W")
    dtW = wstack("", "dt_W")
    convw = wstack("", "conv_w")
    convb = wstack("", "conv_b")
    dtb = wstack("", "dt_b")
    Alog = wstack("", "Alog")
    Dp = wstack("", "D")
    nw = wstack("", "norm_w")
    nb = wstack("", "norm_b")

    inWT = np.ascontiguousarray(inW.transpose(0, 2, 1))
    outWT = np.ascontiguousarray(outW.transpose(0, 2, 1))
    dtWT = np.ascontiguousarray(dtW.transpose(0, 2, 1))
    xpWTp = np.zeros((2 * NL, DIN, 128), np.float32)
    xpWTp[:, :, 0:16] = xpW[:, 64:80].transpose(0, 2, 1)
    xpWTp[:, :, 32:48] = xpW[:, 80:96].transpose(0, 2, 1)
    xpWTp[:, :, 64:128] = xpW[:, 0:64].transpose(0, 2, 1)
    convw_p = np.ascontiguousarray(
        convw.reshape(2 * NL, NDB, 128, K).transpose(0, 2, 1, 3).reshape(2 * NL, 128, NDB * K))

    def packv(v):
        return np.ascontiguousarray(v.reshape(2 * NL, NDB, 128).transpose(0, 2, 1))

    def packd(v):
        return np.ascontiguousarray(v.reshape(2 * NL, NDC, 128).transpose(0, 2, 1))

    Alogp = np.ascontiguousarray(
        Alog.reshape(2 * NL, NDB, 128, S).transpose(0, 2, 1, 3).reshape(2 * NL, 128, NDB * S))
    normfw = np.ascontiguousarray(
        np.asarray(inputs["L0_attn_normf_w"]).reshape(NDC, 128).T)
    normfb = np.ascontiguousarray(
        np.asarray(inputs["L0_attn_normf_b"]).reshape(NDC, 128).T)
    mergeWT = np.ascontiguousarray(np.asarray(inputs["L0_attn_merge_W"]).T)
    mergeb = np.asarray(inputs["L0_attn_merge_b"]).reshape(1, D)
    ffnb = np.asarray(inputs["L0_ffn_bias"]).reshape(1, D)

    ei = np.asarray(inputs["L0_ffn_expert_indices"])
    ew = np.asarray(inputs["L0_ffn_expert_weights"])
    flat_idx = ei.reshape(-1)
    flat_w = ew.reshape(-1)
    order = np.argsort(flat_idx, kind="stable")
    tok = (order // TOPK_CONST).astype(np.int64)
    slot = (order % TOPK_CONST).astype(np.int64)
    xrowsT = np.ascontiguousarray(x[0][tok].T)
    rowscale = np.ascontiguousarray(
        flat_w[order].astype(np.float32).reshape(NROW // 128, 128).T)
    idx0 = np.where(slot == 0, tok, DUMP).astype(np.int32)
    idx1 = np.where(slot == 1, tok, DUMP).astype(np.int32)
    scatidx0 = np.ascontiguousarray(idx0.reshape(NROW // 128, 128).T)
    scatidx1 = np.ascontiguousarray(idx1.reshape(NROW // 128, 128).T)
    counts = np.bincount(flat_idx, minlength=EXPN)
    moe_segs = []
    pos = 0
    for e in range(EXPN):
        end = pos + int(counts[e])
        r = pos
        while r < end:
            r2 = min(r + 512, end)
            moe_segs.append((e, r, r2))
            r = r2
        pos = end

    ident = np.eye(128, dtype=np.float32)
    onesv = np.ones((128, 1), np.float32)

    for c in range(8):
        b, q = c // 4, c % 4
        t_lo, t_hi = q * CORE_T, (q + 1) * CORE_T
        xf = np.zeros((TEXT, D), np.float32)
        lo = max(0, t_lo - W)
        xf[W - (t_lo - lo):W + CORE_T] = x[b, lo:t_hi]
        xfwT = np.ascontiguousarray(xf.T)
        xbw = np.zeros((TEXT, D), np.float32)
        hi = min(T, t_hi + W)
        seg = x[b, t_lo:hi][::-1]
        xbw[TEXT - seg.shape[0]:] = seg
        xbwT = np.ascontiguousarray(xbw.T)
        masks = np.zeros((128, 4), np.float32)
        masks[:, 0] = 0.0 if q == 0 else 1.0
        masks[:, 1] = 0.0 if q == 3 else 1.0
        masks[:, 2] = 1.0 if (b == 0 and q == 0) else 0.0
        maps.append(dict(
            xfwT=xfwT, xbwT=xbwT,
            xown=np.ascontiguousarray(x[b, t_lo:t_hi]),
            inWT=inWT, outWT=outWT, xpWTp=xpWTp, dtWT=dtWT,
            convw=convw_p, convb=packv(convb), dtb=packv(dtb), Dpv=packv(Dp),
            Alogp=Alogp, nwv=packd(nw), nbv=packd(nb),
            normfw=normfw, normfb=normfb,
            mergeWT=mergeWT, mergeb=mergeb, ffnb=ffnb,
            xrowsT=xrowsT,
            w1s=np.asarray(inputs["L0_ffn_w1"]),
            w2s=np.asarray(inputs["L0_ffn_w2"]),
            rowscale=rowscale, scatidx0=scatidx0, scatidx1=scatidx1,
            ident=ident, onesv=onesv, masks=masks,
            constv=np.broadcast_to(np.array([1e-5, -0.5], np.float32), (128, 2)).copy(),
        ))
    return maps, moe_segs


TOPK_CONST = 2


def kernel(**inputs):
    maps, moe_segs = _host_prep(inputs)
    nc, P = build_core_program(moe_segs)
    res = run_bass_kernel_spmd(nc, maps, core_ids=list(range(8)))
    outs = [res.results[c]["out"] for c in range(8)]
    return np.concatenate(outs, axis=0).reshape(B, T, D).astype(np.float32)
